# version 16
# baseline (speedup 1.0000x reference)
import numpy as np
import concourse.bass as bass
import concourse.mybir as mybir
from concourse.bass_utils import run_bass_kernel_spmd

F32 = mybir.dt.float32
BF16 = mybir.dt.bfloat16
AF = mybir.ActivationFunctionType
ALU = mybir.AluOpType
AX = mybir.AxisListType

DEBUG = False

B, L, D = 2, 8192, 1024
NCORES = 8
TOK = 2048
CH = 64
OWN = 16
DC = 8
DEW = 1032
DECN = 9
E = 8
INTER = 2048
TB = 1024
NTB = 2
SCALE = float(np.float32(1.0 / np.sqrt(L)))
EPS = 1e-5
PADCYC = 384


class MK:
    def __init__(self):
        nc = bass.Bass("TRN2", target_bir_lowering=False, detect_race_conditions=False)
        self.nc = nc
        d = nc.dram_tensor
        self.xfull = d("xfull", [L, D], F32, kind="ExternalInput").ap()
        self.qkvT = d("qkvT", [D, 3 * D], F32, kind="ExternalInput").ap()
        self.qkvb = d("qkvb", [1, 3 * D], F32, kind="ExternalInput").ap()
        self.nw = d("nw", [128, 8], F32, kind="ExternalInput").ap()
        self.nb = d("nb", [128, 8], F32, kind="ExternalInput").ap()
        self.projT = d("projT", [D, D], F32, kind="ExternalInput").ap()
        self.projb = d("projb", [1, D], F32, kind="ExternalInput").ap()
        self.gateT = d("gateT", [D, E], F32, kind="ExternalInput").ap()
        self.w13b = d("w13b", [E, 32, 128, 1024], F32, kind="ExternalInput").ap()
        self.w2t = d("w2t", [E, INTER, D], F32, kind="ExternalInput").ap()
        self.eye = d("eye", [128, 128], F32, kind="ExternalInput").ap()
        self.out = d("out", [TOK, D], F32, kind="ExternalOutput").ap()
        self.dbg = {}
        if DEBUG:
            for name, shape, dt in [
                ("d_ht", [128, CH * DEW], BF16), ("d_gs", [128, DECN * DEW], BF16),
                ("d_m1t", [128, DECN * D], BF16), ("d_wmat", [128, 8 * D], BF16),
                ("d_rsw", [128, 8], F32), ("d_vf", [128, 8 * TOK], BF16),
                ("d_at", [128, 8 * TOK], BF16), ("d_y1", [128, OWN * D], F32),
                ("d_lgt", [8, TOK], F32), ("d_comb", [128, OWN * E], F32),
                ("d_wqx", [128, DECN * D], BF16), ("d_wvx", [128, DECN * D], BF16),
                ("d_hof", [128, DECN * TOK], BF16),
            ]:
                self.dbg[name] = d(name, shape, dt, kind="ExternalOutput").ap()

    def alloc(self, ctx):
        nc = self.nc
        ent = ctx.enter_context
        self.eyeb = ent(nc.sbuf_tensor("eyeb", [128, 128], BF16))
        self.eyef = ent(nc.sbuf_tensor("eyef", [128, 128], F32))
        self.nwc = ent(nc.sbuf_tensor("nwc", [128, 8], F32))
        self.nbc = ent(nc.sbuf_tensor("nbc", [128, 8], F32))
        self.epst = ent(nc.sbuf_tensor("epst", [128, 1], F32))
        self.stats = ent(nc.sbuf_tensor("stats", [128, 256], F32))
        self.rsw = ent(nc.sbuf_tensor("rsw", [128, 8], F32))
        self.gatets = ent(nc.sbuf_tensor("gatets", [128, 64], F32))
        self.onesf = ent(nc.sbuf_tensor("onesf", [1, 128], F32))

        SLAB = 193536
        self.slab = ent(nc.sbuf_tensor("slab", [128, SLAB], mybir.dt.uint8))
        base = nc.lookup_mloc(self.slab).addr
        self._ranges = []

        def ov(name, off, shape, dtype):
            per = int(np.prod(shape[1:])) * mybir.dt.size(dtype)
            assert off % 32 == 0, name
            assert off + per <= SLAB, (name, off, per)
            self._ranges.append((name, off, off + per))
            hnd = nc.virtual_sbuf_tensor(name, shape, dtype)
            nc._tensor(hnd.name, hnd.shape, hnd.dtype, type="SB")
            mloc = nc.lookup_mloc(hnd)
            mloc.allocated = True
            mloc.addr = base + off
            return hnd

        self.HT = ov("HT", 0, [128, CH * DEW], BF16)
        self.GS = ov("GS", 132096, [128, DECN * DEW], BF16)
        self.HOF = ov("HOF", 156672, [128, DECN * TOK], BF16)
        self.xbuf = [ov(f"xbuf{i}", 132096 + i * 4096, [128, D], F32) for i in range(6)]
        self.WQX = ov("WQX", 0, [128, DECN * D], BF16)
        self.WKX = ov("WKX", 18432, [128, DECN * D], BF16)
        self.WVX = ov("WVX", 36864, [128, DECN * D], BF16)
        self.M1T = ov("M1T", 55296, [128, DECN * D], BF16)
        self.WMAT = ov("WMAT", 73728, [128, 8 * D], BF16)
        self.WT = ov("WT", 90112, [128, 8 * D], BF16)
        self.qbuf = [ov(f"qbuf{i}", 106496 + i * 12288, [128, 3 * D], F32) for i in range(2)]
        self.brows = ent(nc.sbuf_tensor("brows", [1, 4 * D], F32))
        self.qkvbS = self.brows[0:1, 0:3 * D]
        self.projbR = self.brows[0:1, 3 * D:4 * D]
        self.VF = ov("VF", 0, [128, 8 * TOK], BF16)
        self.AT = ov("AT", 132096, [128, 8 * TOK], BF16)
        self.XOWN = ov("XOWN", 0, [128, OWN * D], F32)
        self.Y1TOKF = ov("Y1TOKF", 65536, [128, OWN * D], F32)
        self.PJ = ov("PJ", 164864, [128, 8 * D], BF16)
        self.projbB = ov("projbB", 181248, [128, D], F32)
        self.z1pre = ov("z1pre", 189440, [128, D], F32)
        self.z1b = ov("z1b", 185344, [128, D], F32)
        self.Y1TB = ov("Y1TB", 0, [128, OWN * D], BF16)
        self.Y1FB = ov("Y1FB", 32768, [128, 8 * TOK], BF16)
        self.LGT = ov("LGT", 131072, [8, TOK], F32)
        self.strip = [ov(f"strip{i}", 139264 + i * 2048, [128, 512], F32) for i in range(2)]
        self.SCO = ov("SCO", 143360, [128, OWN * E], F32)
        self.EXPS = ov("EXPS", 143872, [128, OWN * E], F32)
        self.TKG = ov("TKG", 144384, [128, OWN * 4], F32)
        self.TKG2 = ov("TKG2", 144640, [128, OWN * 4], F32)
        self.TKM = ov("TKM", 144896, [128, OWN * E], F32)
        self.TKM2 = ov("TKM2", 145408, [128, OWN * E], F32)
        self.TKE = ov("TKE", 145920, [128, OWN * E], F32)
        self.TKC1 = ov("TKC1", 146432, [128, OWN * E], F32)
        self.TKS = ov("TKS", 146944, [128, OWN * 8], F32)
        self.COMB = ov("COMB", 147456, [128, OWN * E], F32)
        self.ACT = ov("ACT", 65536, [128, 16 * TB], BF16)
        self.OUT2 = [ov("OUT2a", 98304, [128, 8 * D], BF16),
                     ov("OUT2b", 114688, [128, 8 * D], BF16)]
        self.W2ALL = ov("W2ALL", 148480, [128, 16 * 1024], BF16)
        self.W13B = [ov(f"W13B{i}", 181248 + i * 2048, [128, 1024], BF16) for i in range(4)]
        self.SG = ov("SG", 189440, [128, 2048], BF16)
        self.Z2 = [ov("Z2x", 143360, [128, D], F32)]
        self.ln3scr = ov("ln3scr", 139264, [128, D], F32)
        self.OUTB = [ov(f"OUTB{i}", 131072 + i * 4096, [128, D], F32) for i in range(2)]

        self.pgen = [ent(nc.psum_tensor(f"pg{i}", [128, 512], F32)) for i in range(6)]
        self.pbank = [ent(nc.psum_tensor(f"pb{i}", [128, 512], F32)) for i in range(2)]

        def pov(name, i, shape, dtype):
            h = nc.virtual_psum_tensor(name, shape, dtype)
            nc._tensor(h.name, h.shape, h.dtype, type="PSUM")
            m = nc.lookup_mloc(h)
            m.allocated = True
            m.bank = nc.lookup_mloc(self.pbank[i]).bank
            m.addr = 0
            return h

        self.pt = [pov(f"ptx{i}", i, [128, 128], BF16) for i in range(2)]
        self.ptf = [pov(f"ptfx{i}", i, [128, 128], F32) for i in range(2)]

        def pov2(name, src_t, shape, dtype):
            h = nc.virtual_psum_tensor(name, shape, dtype)
            nc._tensor(h.name, h.shape, h.dtype, type="PSUM")
            m = nc.lookup_mloc(h)
            m.allocated = True
            m.bank = nc.lookup_mloc(src_t).bank
            m.addr = 0
            return h

        self.pgc = [pov2(f"pgcx{i}", self.pgen[4 + i], [128, 16], F32) for i in range(2)]
        sem = lambda n: ent(nc.semaphore(n))
        self.sT, self.sV, self.sS = sem("sT"), sem("sV"), sem("sS")
        self.sQS, self.sQG, self.sST = sem("sQS"), sem("sQG"), sem("sST")
        self.sVx, self.sSx = sem("sVx"), sem("sSx")
        self.vx_cnt, self.sx_cnt = 0, 0

    def vsync(self, v, inst):
        self.vx_cnt += 1
        inst.then_inc(self.sVx, 1)
        v.wait_ge(self.sVx, self.vx_cnt)

    def ssync(self, s, inst):
        self.sx_cnt += 1
        inst.then_inc(self.sSx, 1)
        s.wait_ge(self.sSx, self.sx_cnt)

    def dump(self, blk, name, src_ap, wait_sem, wait_val):
        if not DEBUG or name not in self.dbg:
            return
        dst = self.dbg[name]
        st0 = self.stC

        def _sy(sy):
            sy.wait_ge(wait_sem, wait_val)
            sy.dma_start(dst, src_ap).then_inc(self.sST, 16)
            sy.wait_ge(self.sST, st0 + 16)
        blk.sync(_sy)
        self.stC = st0 + 16

    def build(self):
        from contextlib import ExitStack
        with ExitStack() as ctx:
            self.alloc(ctx)
            blk = ctx.enter_context(self.nc.Block())
            self.stC = 0

            qs = 0

            def _sy0(sy):
                sy.dma_start(self.eyef[:], self.eye).then_inc(self.sQS, 16)
                sy.dma_start(self.nwc[:], self.nw).then_inc(self.sQS, 16)
                sy.dma_start(self.nbc[:], self.nb).then_inc(self.sQS, 16)
                src = self.gateT.rearrange("(c p) e -> p c e", p=128)
                dst = self.gatets.rearrange("p (c e) -> p c e", c=8)
                sy.dma_start(dst, src).then_inc(self.sQS, 16)
                sy.dma_start(self.qkvbS, self.qkvb).then_inc(self.sQS, 16)
                sy.dma_start(self.projbR, self.projb).then_inc(self.sQS, 16)
            blk.sync(_sy0)
            qs += 6 * 16
            qg = 0

            def _g0(g):
                g.dma_start(self.eyeb[:], self.eye).then_inc(self.sQG, 16)
            blk.gpsimd(_g0)
            qg += 16

            vC = 0

            def _v0(v):
                v.memset(self.epst[:], EPS)
                v.memset(self.onesf[:], 1.0)
                v.memset(self.HT[:, 0:CH * DEW // 2], 0.0)
                v.memset(self.HT[:, CH * DEW // 2:], 0.0)
                ones_cols = self.HT.rearrange("p (c w) -> p c w", c=CH)[:, :, 1024:1025]
                v.memset(ones_cols, 1.0).then_inc(self.sV, 1)
            blk.vector(_v0)
            vC += 1

            qs1 = qs
            v1 = vC
            ST = self.stats
            lngr = [(g * 3, min(g * 3 + 3, CH)) for g in range((CH + 2) // 3)]
            vA, vB, v_norm = {}, {}, {}
            vv = v1
            for gi, (a, b) in enumerate(lngr):
                for cc in range(a, b):
                    vv += 1; vA[cc] = vv
                vv += 1; vB[gi] = vv
                if gi >= 1:
                    for cc in range(*lngr[gi - 1]):
                        vv += 1; v_norm[cc] = vv
            for cc in range(*lngr[-1]):
                vv += 1; v_norm[cc] = vv
            sA, sR = {}, {}
            ss = 0
            for gi, (a, b) in enumerate(lngr):
                for cc in range(a, b):
                    ss += 1; sA[cc] = ss
                ss += 1; sR[gi] = ss

            def _sy1(sy):
                for cc in range(CH):
                    if cc >= 6:
                        sy.wait_ge(self.sV, v_norm[cc - 6])
                    sy.dma_start(self.xbuf[cc % 6][:],
                                 self.xfull[cc * 128:(cc + 1) * 128, :]).then_inc(self.sQS, 16)
            blk.sync(_sy1)
            qs = qs1 + CH * 16

            def _v1(v):
                for gi, (a, b) in enumerate(lngr):
                    for cc in range(a, b):
                        v.wait_ge(self.sQS, qs1 + (cc + 1) * 16)
                        v.reduce_sum(ST[:, cc:cc + 1], self.xbuf[cc % 6][:], axis=AX.X
                                     ).then_inc(self.sV, 1)
                    v.wait_ge(self.sS, sA[b - 1])
                    self.vsync(v, v.tensor_scalar_mul(ST[:, 128 + a:128 + b],
                                                      ST[:, a:b], -1.0 / D))
                    v.tensor_mul(ST[:, a:b], ST[:, 128 + a:128 + b], ST[:, 128 + a:128 + b])
                    self.vsync(v, v.tensor_scalar_mul(ST[:, 192 + a:192 + b],
                                                      ST[:, 64 + a:64 + b], 1.0 / D))
                    v.tensor_sub(ST[:, 64 + a:64 + b], ST[:, 192 + a:192 + b],
                                 ST[:, a:b]).then_inc(self.sV, 1)
                    if gi >= 1:
                        v.wait_ge(self.sS, sR[gi - 1])
                        for cc in range(*lngr[gi - 1]):
                            v.tensor_scalar(
                                out=self.HT[:, cc * DEW:cc * DEW + D], in0=self.xbuf[cc % 6][:],
                                scalar1=ST[:, 128 + cc:129 + cc], scalar2=ST[:, 192 + cc:193 + cc],
                                op0=ALU.add, op1=ALU.mult,
                            ).then_inc(self.sV, 1)
                v.wait_ge(self.sS, sR[len(lngr) - 1])
                for cc in range(*lngr[-1]):
                    v.tensor_scalar(
                        out=self.HT[:, cc * DEW:cc * DEW + D], in0=self.xbuf[cc % 6][:],
                        scalar1=ST[:, 128 + cc:129 + cc], scalar2=ST[:, 192 + cc:193 + cc],
                        op0=ALU.add, op1=ALU.mult,
                    ).then_inc(self.sV, 1)
            blk.vector(_v1)
            vC = vv

            def _s1(s):
                for gi, (a, b) in enumerate(lngr):
                    for cc in range(a, b):
                        s.wait_ge(self.sQS, qs1 + (cc + 1) * 16)
                        s.activation(self.HT[:, cc * DEW:cc * DEW + D], self.xbuf[cc % 6][:],
                                     AF.Square, accum_out=ST[:, 64 + cc:65 + cc]
                                     ).then_inc(self.sS, 1)
                    s.wait_ge(self.sV, vB[gi])
                    self.ssync(s, s.activation(ST[:, a:b], ST[:, 64 + a:64 + b], AF.Ln,
                                               bias=self.epst[:], scale=1.0))
                    s.activation(ST[:, 192 + a:192 + b], ST[:, a:b], AF.Exp, scale=-0.5
                                 ).then_inc(self.sS, 1)
            blk.scalar(_s1)
            sC = ss
            tC = 0
            t3, v3 = tC, vC

            def _t3(t):
                for i in range(OWN * 8):
                    cc, dcx = divmod(i, 8)
                    t.wait_ge(self.sV, v_norm[cc])
                    if i >= 2:
                        t.wait_ge(self.sV, v3 + i - 1)
                    t.transpose(self.pt[i % 2][:],
                                self.HT[:, cc * DEW + dcx * 128: cc * DEW + dcx * 128 + 128],
                                self.eyeb[:]).then_inc(self.sT, 1)
            blk.tensor(_t3)
            tC = t3 + OWN * 8

            def _v3(v):
                for i in range(OWN * 8):
                    cc, dcx = divmod(i, 8)
                    v.wait_ge(self.sT, t3 + i + 1)
                    v.tensor_copy(
                        self.HOF[:, dcx * TOK + cc * 128: dcx * TOK + cc * 128 + 128],
                        self.pt[i % 2][:]).then_inc(self.sV, 1)
                v.memset(self.HOF[0:8, 8 * TOK: 9 * TOK], 0.0)
                v.memset(self.HOF[0:1, 8 * TOK: 9 * TOK], 1.0).then_inc(self.sV, 1)
            blk.vector(_v3)
            vC = v3 + OWN * 8 + 1
            v_hof = vC
            self.dump(blk, "d_hof", self.HOF[:], self.sV, v_hof)

            t2, v2 = tC, vC

            def _t2(t):
                t.wait_ge(self.sV, v_norm[CH - 1])
                for mc in range(DECN):
                    M = 128 if mc < 8 else 8
                    pa = self.pgen[2 * (mc % 2)]
                    pb = self.pgen[2 * (mc % 2) + 1]
                    pc_ = self.pgc[mc % 2]
                    if mc >= 2:
                        t.wait_ge(self.sV, v2 + mc - 1)
                    for cc in range(CH):
                        lhs = self.HT[:, cc * DEW + mc * 128: cc * DEW + mc * 128 + M]
                        st_, sp = (cc == 0), (cc == CH - 1)
                        t.matmul(pa[0:M, :], lhs, self.HT[:, cc * DEW: cc * DEW + 512],
                                 start=st_, stop=sp)
                        t.matmul(pb[0:M, :], lhs, self.HT[:, cc * DEW + 512: cc * DEW + 1024],
                                 start=st_, stop=sp)
                        mm = t.matmul(pc_[0:M, 0:8], lhs,
                                      self.HT[:, cc * DEW + 1024: cc * DEW + 1032],
                                      start=st_, stop=sp)
                        if sp:
                            mm.then_inc(self.sT, 1)
            blk.tensor(_t2)
            tC = t2 + DECN

            def _v2(v):
                for mc in range(DECN):
                    M = 128 if mc < 8 else 8
                    pa = self.pgen[2 * (mc % 2)]
                    pb = self.pgen[2 * (mc % 2) + 1]
                    pc_ = self.pgc[mc % 2]
                    v.wait_ge(self.sT, t2 + mc + 1)
                    o = mc * DEW
                    v.tensor_copy(self.GS[0:M, o:o + 512], pa[0:M, :])
                    v.tensor_copy(self.GS[0:M, o + 512:o + 1024], pb[0:M, :])
                    v.tensor_copy(self.GS[0:M, o + 1024:o + 1032], pc_[0:M, 0:8]
                                  ).then_inc(self.sV, 1)
            blk.vector(_v2)
            vC = v2 + DECN
            v_g = vC
            self.dump(blk, "d_gs", self.GS[:], self.sV, v_g)

            qs4, v4, t4 = qs, vC, tC

            def _sy4(sy):
                sy.wait_ge(self.sT, t2 + DECN)
                for dcx in range(DC):
                    if dcx >= 2:
                        sy.wait_ge(self.sV, v4 + dcx - 1)
                        sy.wait_ge(self.sT, t4 + dcx - 1)
                    sy.dma_start(self.qbuf[dcx % 2][:],
                                 self.qkvT[dcx * 128:(dcx + 1) * 128, :]).then_inc(self.sQS, 16)
            blk.sync(_sy4)
            qs = qs4 + DC * 16

            def _v4(v):
                v.wait_ge(self.sT, t2 + DECN)
                if DEBUG:
                    v.wait_ge(self.sST, self.stC)
                for dcx in range(DC):
                    qb = self.qbuf[dcx % 2]
                    v.wait_ge(self.sQS, qs4 + (dcx + 1) * 16)
                    g = self.nwc[:, dcx:dcx + 1]
                    v.tensor_scalar(out=self.WQX[:, dcx * D:(dcx + 1) * D], in0=qb[:, 0:D],
                                    scalar1=g, scalar2=None, op0=ALU.mult)
                    v.tensor_scalar(out=self.WKX[:, dcx * D:(dcx + 1) * D], in0=qb[:, D:2 * D],
                                    scalar1=g, scalar2=None, op0=ALU.mult)
                    v.tensor_scalar(out=self.WVX[:, dcx * D:(dcx + 1) * D],
                                    in0=qb[:, 2 * D:3 * D],
                                    scalar1=g, scalar2=None, op0=ALU.mult).then_inc(self.sV, 1)
            blk.vector(_v4)
            vC = v4 + DC

            def _t4(t):
                for dcx in range(DC):
                    t.wait_ge(self.sQS, qs4 + (dcx + 1) * 16)
                    for j in range(6):
                        mm = t.matmul(self.pgen[j][0:1, :], self.nbc[:, dcx:dcx + 1],
                                      self.qbuf[dcx % 2][:, j * 512:(j + 1) * 512],
                                      start=(dcx == 0), stop=(dcx == DC - 1))
                        if j == 5:
                            mm.then_inc(self.sT, 1)
            blk.tensor(_t4)
            tC = t4 + DC

            def _v4b(v):
                v.wait_ge(self.sT, t4 + DC)
                for w in (self.WQX, self.WKX, self.WVX):
                    v.memset(w[0:8, 8 * D: 9 * D], 0.0)
                for j, (w, boff) in enumerate([(self.WQX, 0), (self.WQX, 512),
                                               (self.WKX, 1024), (self.WKX, 1536),
                                               (self.WVX, 2048), (self.WVX, 2560)]):
                    col = boff % 1024
                    v.tensor_add(w[0:1, 8 * D + col: 8 * D + col + 512],
                                 self.pgen[j][0:1, :], self.brows[0:1, boff:boff + 512]
                                 ).then_inc(self.sV, 1 if j == 5 else 1)
            blk.vector(_v4b)
            vC += 6
            v_wx = vC
            self.dump(blk, "d_wqx", self.WQX[:], self.sV, v_wx)
            self.dump(blk, "d_wvx", self.WVX[:], self.sV, v_wx)

            t5, v5 = tC, vC

            def _t5(t):
                t.wait_ge(self.sV, v_wx)
                for rc in range(DECN):
                    Mr = 128 if rc < 8 else 8
                    if rc >= 2:
                        t.wait_ge(self.sV, v5 + rc - 1)
                    for ns in range(2):
                        p = self.pgen[2 * (rc % 2) + ns]
                        for dcx in range(DECN):
                            K = 128 if dcx < 8 else 8
                            lhs = self.GS[0:K, dcx * DEW + rc * 128: dcx * DEW + rc * 128 + Mr]
                            rhs = self.WQX[0:K, dcx * D + ns * 512: dcx * D + ns * 512 + 512]
                            mm = t.matmul(p[0:Mr, :], lhs, rhs,
                                          start=(dcx == 0), stop=(dcx == DECN - 1))
                            if ns == 1 and dcx == DECN - 1:
                                mm.then_inc(self.sT, 1)
            blk.tensor(_t5)
            tC = t5 + DECN

            def _v5(v):
                for rc in range(DECN):
                    Mr = 128 if rc < 8 else 8
                    v.wait_ge(self.sT, t5 + rc + 1)
                    for ns in range(2):
                        p = self.pgen[2 * (rc % 2) + ns]
                        cp = v.tensor_copy(
                            self.M1T[0:Mr, rc * D + ns * 512: rc * D + ns * 512 + 512],
                            p[0:Mr, :])
                        if ns == 1:
                            cp.then_inc(self.sV, 1)
            blk.vector(_v5)
            vC = v5 + DECN
            v_m1 = vC
            self.dump(blk, "d_m1t", self.M1T[:], self.sV, v_m1)

            t6, v6, s6 = tC, vC, sC

            def _t6(t):
                t.wait_ge(self.sV, v_m1)
                for ic in range(8):
                    if ic >= 2:
                        t.wait_ge(self.sS, s6 + ic - 1)
                    for ns in range(2):
                        p = self.pgen[2 * (ic % 2) + ns]
                        for dcx in range(DECN):
                            K = 128 if dcx < 8 else 8
                            lhs = self.M1T[0:K, dcx * D + ic * 128: dcx * D + ic * 128 + 128]
                            rhs = self.WKX[0:K, dcx * D + ns * 512: dcx * D + ns * 512 + 512]
                            mm = t.matmul(p[:], lhs, rhs,
                                          start=(dcx == 0), stop=(dcx == DECN - 1))
                            if ns == 1 and dcx == DECN - 1:
                                mm.then_inc(self.sT, 1)
            blk.tensor(_t6)
            tC = t6 + 8

            def _v6(v):
                for ic in range(8):
                    pa = self.pgen[2 * (ic % 2)]
                    pb = self.pgen[2 * (ic % 2) + 1]
                    v.wait_ge(self.sT, t6 + ic + 1)
                    v.reduce_max(ST[:, 20:21], pa[:], axis=AX.X, negate=True)
                    self.vsync(v, v.reduce_max(ST[:, 21:22], pb[:], axis=AX.X, negate=True))
                    self.vsync(v, v.tensor_tensor(ST[:, 22:23], ST[:, 20:21],
                                                  ST[:, 21:22], ALU.min))
                    v.tensor_scalar_mul(ST[:, 24:25], ST[:, 22:23], SCALE).then_inc(self.sV, 1)
                    v.wait_ge(self.sS, s6 + ic + 1)
                    self.vsync(v, v.reduce_sum(ST[:, 23:24],
                                               self.WMAT[:, ic * D:(ic + 1) * D], axis=AX.X))
                    v.reciprocal(self.rsw[:, ic:ic + 1], ST[:, 23:24]).then_inc(self.sV, 1)
            blk.vector(_v6)
            vC = v6 + 16

            def _s6(s):
                for ic in range(8):
                    pa = self.pgen[2 * (ic % 2)]
                    pb = self.pgen[2 * (ic % 2) + 1]
                    s.wait_ge(self.sV, v6 + 2 * ic + 1)
                    s.activation(self.WMAT[:, ic * D: ic * D + 512], pa[:], AF.Exp,
                                 bias=ST[:, 24:25], scale=SCALE)
                    s.activation(self.WMAT[:, ic * D + 512: ic * D + 1024], pb[:], AF.Exp,
                                 bias=ST[:, 24:25], scale=SCALE).then_inc(self.sS, 1)
            blk.scalar(_s6)
            sC = s6 + 8
            v_soft = vC
            self.dump(blk, "d_wmat", self.WMAT[:], self.sS, sC)
            self.dump(blk, "d_rsw", self.rsw[:], self.sV, v_soft)

            t7, v7 = tC, vC

            def _t7(t):
                t.wait_ge(self.sV, v_soft)
                for i in range(64):
                    ic, jc = divmod(i, 8)
                    if i >= 2:
                        t.wait_ge(self.sV, v7 + i - 1)
                    t.transpose(self.pt[i % 2][:],
                                self.WMAT[:, ic * D + jc * 128: ic * D + jc * 128 + 128],
                                self.eyeb[:]).then_inc(self.sT, 1)
            blk.tensor(_t7)
            tC = t7 + 64

            def _v7(v):
                for i in range(64):
                    ic, jc = divmod(i, 8)
                    v.wait_ge(self.sT, t7 + i + 1)
                    v.tensor_copy(self.WT[:, jc * D + ic * 128: jc * D + ic * 128 + 128],
                                  self.pt[i % 2][:]).then_inc(self.sV, 1)
            blk.vector(_v7)
            vC = v7 + 64
            v_wt = vC

            t8, v8 = tC, vC

            def _t8(t):
                t.wait_ge(self.sV, v_wt)
                for i in range(32):
                    vc_, cs = divmod(i, 4)
                    if i >= 2:
                        t.wait_ge(self.sV, v8 + i - 1)
                    p = self.pgen[4 + (i % 2)]
                    for dcx in range(DECN):
                        K = 128 if dcx < 8 else 8
                        lhs = self.WVX[0:K, dcx * D + vc_ * 128: dcx * D + vc_ * 128 + 128]
                        rhs = self.HOF[0:K, dcx * TOK + cs * 512: dcx * TOK + cs * 512 + 512]
                        mm = t.matmul(p[:], lhs, rhs, start=(dcx == 0), stop=(dcx == DECN - 1))
                        if dcx == DECN - 1:
                            mm.then_inc(self.sT, 1)
            blk.tensor(_t8)
            tC = t8 + 32

            def _v8(v):
                if DEBUG:
                    v.wait_ge(self.sST, self.stC)
                for i in range(32):
                    vc_, cs = divmod(i, 4)
                    v.wait_ge(self.sT, t8 + i + 1)
                    p = self.pgen[4 + (i % 2)]
                    v.tensor_copy(self.VF[:, vc_ * TOK + cs * 512: vc_ * TOK + cs * 512 + 512],
                                  p[:]).then_inc(self.sV, 1)
            blk.vector(_v8)
            vC = v8 + 32
            v_vf = vC
            self.dump(blk, "d_vf", self.VF[:], self.sV, v_vf)

            t9, v9 = tC, vC

            def _t9(t):
                t.wait_ge(self.sV, v_vf)
                if DEBUG:
                    t.wait_ge(self.sST, self.stC)
                for i in range(32):
                    ic, cs = divmod(i, 4)
                    if i >= 4:
                        t.wait_ge(self.sV, v9 + i - 3)
                    p = self.pgen[i % 4]
                    for jc in range(8):
                        lhs = self.WT[:, jc * D + ic * 128: jc * D + ic * 128 + 128]
                        rhs = self.VF[:, jc * TOK + cs * 512: jc * TOK + cs * 512 + 512]
                        mm = t.matmul(p[:], lhs, rhs, start=(jc == 0), stop=(jc == 7))
                        if jc == 7:
                            mm.then_inc(self.sT, 1)
            blk.tensor(_t9)
            tC = t9 + 32

            def _v9(v):
                if DEBUG:
                    v.wait_ge(self.sST, self.stC)
                for i in range(32):
                    ic, cs = divmod(i, 4)
                    v.wait_ge(self.sT, t9 + i + 1)
                    p = self.pgen[i % 4]
                    v.tensor_scalar(
                        out=self.AT[:, ic * TOK + cs * 512: ic * TOK + cs * 512 + 512],
                        in0=p[:], scalar1=self.rsw[:, ic:ic + 1], scalar2=None,
                        op0=ALU.mult).then_inc(self.sV, 1)
            blk.vector(_v9)
            vC = v9 + 32
            v_at = vC
            self.dump(blk, "d_at", self.AT[:], self.sV, v_at)

            t10, v10, s10, qs10, qg10 = tC, vC, sC, qs, qg

            def _sy10(sy):
                sy.wait_ge(self.sV, v_at)
                if DEBUG:
                    sy.wait_ge(self.sST, self.stC)
                for cc in range(OWN):
                    sy.dma_start(self.XOWN[:, cc * D:(cc + 1) * D],
                                 self.xfull[cc * 128:(cc + 1) * 128, :]).then_inc(self.sQS, 16)
            blk.sync(_sy10)
            qs = qs10 + OWN * 16

            def _g10(g):
                g.wait_ge(self.sV, v_vf)
                if DEBUG:
                    g.wait_ge(self.sST, self.stC)
                for qc in range(8):
                    g.dma_start(self.PJ[:, qc * D:(qc + 1) * D],
                                self.projT[qc * 128:(qc + 1) * 128, :]).then_inc(self.sQG, 16)
            blk.gpsimd(_g10)
            qg = qg10 + 8 * 16

            v_st10 = {cc: v10 + 1 + 2 * cc + 1 for cc in range(OWN)}
            v_nm10 = {cc: v10 + 1 + 2 * cc + 2 for cc in range(OWN)}
            s_e10 = {cc: s10 + cc + 1 for cc in range(OWN)}

            def _t10(t):
                t.wait_ge(self.sV, v_at)
                t.wait_ge(self.sQS, qs10 + 16)
                for ns in range(2):
                    mm = t.matmul(self.pgen[4 + ns][:], self.onesf[:],
                                  self.brows[0:1, 3 * D + ns * 512: 3 * D + (ns + 1) * 512],
                                  start=True, stop=True)
                    mm.then_inc(self.sT, 1)
                t.wait_ge(self.sQG, qg10 + 8 * 16)
                for cc in range(OWN):
                    if cc >= 2:
                        t.wait_ge(self.sV, v_st10[cc - 2])
                    for ns in range(2):
                        p = self.pgen[2 * (cc % 2) + ns]
                        for qc in range(8):
                            lhs = self.AT[:, qc * TOK + cc * 128: qc * TOK + cc * 128 + 128]
                            rhs = self.PJ[:, qc * D + ns * 512: qc * D + ns * 512 + 512]
                            mm = t.matmul(p[:], lhs, rhs, start=(qc == 0), stop=(qc == 7))
                            if ns == 1 and qc == 7:
                                mm.then_inc(self.sT, 1)
            blk.tensor(_t10)
            tC = t10 + 2 + OWN

            def _v10(v):
                v.wait_ge(self.sT, t10 + 2)
                v.tensor_copy(self.projbB[:, 0:512], self.pgen[4][:])
                v.tensor_copy(self.projbB[:, 512:1024], self.pgen[5][:]).then_inc(self.sV, 1)
                for cc in range(OWN):
                    r = cc % 4
                    v.wait_ge(self.sT, t10 + 2 + cc + 1)
                    v.wait_ge(self.sQS, qs10 + (cc + 1) * 16)
                    xo = self.XOWN[:, cc * D:(cc + 1) * D]
                    for ns in range(2):
                        p = self.pgen[2 * (cc % 2) + ns]
                        v.scalar_tensor_tensor(
                            out=self.z1pre[:, ns * 512:(ns + 1) * 512],
                            in0=xo[:, ns * 512:(ns + 1) * 512], scalar=2.0,
                            in1=p[:], op0=ALU.mult, op1=ALU.add)
                    v.tensor_add(self.z1b[:], self.z1pre[:], self.projbB[:])
                    self.vsync(v, v.reduce_sum(ST[:, 32 + r:33 + r], self.z1b[:], axis=AX.X))
                    self.vsync(v, v.tensor_scalar_mul(ST[:, 36 + r:37 + r], ST[:, 32 + r:33 + r], 1.0 / D))
                    v.scalar_tensor_tensor(
                        out=self.z1pre[:], in0=self.z1b[:], scalar=ST[:, 36 + r:37 + r],
                        in1=self.z1b[:], op0=ALU.subtract, op1=ALU.mult,
                        accum_out=ST[:, 40 + r:41 + r],
                    ).then_inc(self.sV, 1)
                    v.wait_ge(self.sS, s_e10[cc])
                    v.tensor_scalar(
                        out=self.Y1TOKF[:, cc * D:(cc + 1) * D], in0=self.z1b[:],
                        scalar1=ST[:, 36 + r:37 + r], scalar2=ST[:, 48 + r:49 + r],
                        op0=ALU.subtract, op1=ALU.mult,
                    ).then_inc(self.sV, 1)
            blk.vector(_v10)
            vC = v10 + 1 + 2 * OWN

            def _s10(s):
                for cc in range(OWN):
                    r = cc % 4
                    s.wait_ge(self.sV, v_st10[cc])
                    self.ssync(s, s.activation(ST[:, 44 + r:45 + r], ST[:, 40 + r:41 + r],
                                                AF.Ln, bias=self.epst[:], scale=1.0 / D))
                    s.activation(ST[:, 48 + r:49 + r], ST[:, 44 + r:45 + r], AF.Exp,
                                 scale=-0.5).then_inc(self.sS, 1)
            blk.scalar(_s10)
            sC = s10 + OWN
            v_y1 = vC
            self.dump(blk, "d_y1", self.Y1TOKF[:], self.sV, v_y1)

            t11, v11, s11 = tC, vC, sC
            tr_mark = lambda g, k: t11 + g * 5 + k + 1
            mm_mark = lambda g: t11 + g * 5 + 5
            lt_mark = lambda tc: t11 + 160 + tc + 1
            vcast = v11 + 1
            vcp_mark = lambda g, k: vcast + g * 4 + k + 1
            vlgt_mark = lambda i: vcast + 128 + i + 1
            vsco_mark = lambda tc: vcast + 132 + tc + 1

            def _t11(t):
                t.wait_ge(self.sV, v_y1)
                for g in range(32):
                    ts4, dcx = divmod(g, 8)
                    for k in range(4):
                        tc = ts4 * 4 + k
                        i = g * 4 + k
                        if i >= 2:
                            t.wait_ge(self.sV, vcp_mark(*divmod(i - 2, 4)))
                        t.transpose(self.ptf[i % 2][:],
                                    self.Y1TOKF[:, tc * D + dcx * 128: tc * D + dcx * 128 + 128],
                                    self.eyef[:]).then_inc(self.sT, 1)
                    t.wait_ge(self.sV, vcp_mark(g, 3))
                    t.matmul(self.pgen[2 + ts4][0:8, :], self.gatets[:, dcx * 8:(dcx + 1) * 8],
                             self.strip[dcx % 2][:], start=(dcx == 0), stop=(dcx == 7)
                             ).then_inc(self.sT, 1)
                for tc in range(OWN):
                    t.wait_ge(self.sV, vlgt_mark(3))
                    if tc >= 2:
                        t.wait_ge(self.sV, vsco_mark(tc - 2))
                    t.transpose(self.ptf[tc % 2][:, 0:8],
                                self.LGT[0:8, tc * 128:(tc + 1) * 128],
                                self.eyef[0:8, 0:8]).then_inc(self.sT, 1)
            blk.tensor(_t11)
            tC = t11 + 160 + OWN

            def _v11(v):
                if DEBUG:
                    v.wait_ge(self.sST, self.stC)
                v.wait_ge(self.sV, v_y1)
                v.tensor_copy(self.Y1TB[:], self.Y1TOKF[:]).then_inc(self.sV, 1)
                for g in range(32):
                    ts4, dcx = divmod(g, 8)
                    if g >= 2:
                        v.wait_ge(self.sT, mm_mark(g - 2))
                    for k in range(4):
                        tc = ts4 * 4 + k
                        v.wait_ge(self.sT, tr_mark(g, k))
                        p = self.ptf[(g * 4 + k) % 2]
                        v.tensor_copy(self.strip[dcx % 2][:, k * 128:(k + 1) * 128], p[:])
                        v.tensor_copy(
                            self.Y1FB[:, dcx * TOK + tc * 128: dcx * TOK + tc * 128 + 128],
                            p[:]).then_inc(self.sV, 1)
                for i in range(4):
                    v.wait_ge(self.sT, mm_mark(i * 8 + 7))
                    v.tensor_copy(self.LGT[0:8, i * 512:(i + 1) * 512],
                                  self.pgen[2 + i][0:8, :]).then_inc(self.sV, 1)
                for tc in range(OWN):
                    v.wait_ge(self.sT, lt_mark(tc))
                    v.tensor_copy(self.SCO[:, tc * 8:(tc + 1) * 8],
                                  self.ptf[tc % 2][:, 0:8]).then_inc(self.sV, 1)
            blk.vector(_v11)
            vC = vcast + 128 + 4 + OWN
            v_sco = vC
            self.dump(blk, "d_lgt", self.LGT[:], self.sV, vcast + 132)

            def _s11(s):
                s.wait_ge(self.sV, v_sco)
                s.activation(self.EXPS[:], self.SCO[:], AF.Exp).then_inc(self.sS, 1)
            blk.scalar(_s11)
            sC = s11 + 1

            TK = self.TKS

            def _v11b(v):
                v.wait_ge(self.sS, s11 + 1)
                for tc in range(OWN):
                    sce = self.EXPS[:, tc * 8:(tc + 1) * 8]
                    v.tensor_max(self.TKG[:, tc * 4:(tc + 1) * 4], sce[:, 0:8:2], sce[:, 1:8:2])
                for tc in range(OWN):
                    v.reduce_max(TK[:, tc * 8:tc * 8 + 1], self.TKG[:, tc * 4:(tc + 1) * 4],
                                 axis=AX.X)
                for tc in range(OWN):
                    ge = self.TKG[:, tc * 4:(tc + 1) * 4]
                    v.scalar_tensor_tensor(out=self.TKG2[:, tc * 4:(tc + 1) * 4], in0=ge,
                                           scalar=TK[:, tc * 8:tc * 8 + 1], in1=ge,
                                           op0=ALU.is_lt, op1=ALU.mult)
                for tc in range(OWN):
                    v.reduce_max(TK[:, tc * 8 + 1:tc * 8 + 2],
                                 self.TKG2[:, tc * 4:(tc + 1) * 4], axis=AX.X)
                for tc in range(OWN):
                    v.tensor_scalar(out=self.TKG2[:, tc * 4:(tc + 1) * 4],
                                    in0=self.TKG[:, tc * 4:(tc + 1) * 4],
                                    scalar1=TK[:, tc * 8 + 1:tc * 8 + 2], scalar2=None,
                                    op0=ALU.is_ge)
                for tc in range(OWN):
                    em = self.TKE[:, tc * 8:(tc + 1) * 8]
                    gm = self.TKG2[:, tc * 4:(tc + 1) * 4]
                    v.tensor_copy(em[:, 0:8:2], gm)
                    v.tensor_copy(em[:, 1:8:2], gm)
                for tc in range(OWN):
                    v.tensor_mul(self.TKM[:, tc * 8:(tc + 1) * 8],
                                 self.EXPS[:, tc * 8:(tc + 1) * 8],
                                 self.TKE[:, tc * 8:(tc + 1) * 8])
                for tc in range(OWN):
                    v.reduce_max(TK[:, tc * 8 + 2:tc * 8 + 3],
                                 self.TKM[:, tc * 8:(tc + 1) * 8], axis=AX.X)
                for tc in range(OWN):
                    mk = self.TKM[:, tc * 8:(tc + 1) * 8]
                    v.scalar_tensor_tensor(out=self.TKM2[:, tc * 8:(tc + 1) * 8], in0=mk,
                                           scalar=TK[:, tc * 8 + 2:tc * 8 + 3], in1=mk,
                                           op0=ALU.is_lt, op1=ALU.mult)
                for tc in range(OWN):
                    v.reduce_max(TK[:, tc * 8 + 3:tc * 8 + 4],
                                 self.TKM2[:, tc * 8:(tc + 1) * 8], axis=AX.X)
                for tc in range(OWN):
                    v.tensor_add(TK[:, tc * 8 + 4:tc * 8 + 5], TK[:, tc * 8 + 2:tc * 8 + 3],
                                 TK[:, tc * 8 + 3:tc * 8 + 4])
                for tc in range(OWN):
                    v.reciprocal(TK[:, tc * 8 + 5:tc * 8 + 6], TK[:, tc * 8 + 4:tc * 8 + 5])
                for tc in range(OWN):
                    v.tensor_mul(TK[:, tc * 8 + 6:tc * 8 + 7], TK[:, tc * 8 + 2:tc * 8 + 3],
                                 TK[:, tc * 8 + 5:tc * 8 + 6])
                for tc in range(OWN):
                    v.tensor_mul(TK[:, tc * 8 + 7:tc * 8 + 8], TK[:, tc * 8 + 3:tc * 8 + 4],
                                 TK[:, tc * 8 + 5:tc * 8 + 6])
                for tc in range(OWN):
                    v.tensor_scalar(out=self.TKC1[:, tc * 8:(tc + 1) * 8],
                                    in0=self.TKM[:, tc * 8:(tc + 1) * 8],
                                    scalar1=TK[:, tc * 8 + 2:tc * 8 + 3],
                                    scalar2=TK[:, tc * 8 + 6:tc * 8 + 7],
                                    op0=ALU.is_equal, op1=ALU.mult)
                for tc in range(OWN):
                    v.tensor_scalar(out=self.TKE[:, tc * 8:(tc + 1) * 8],
                                    in0=self.TKM2[:, tc * 8:(tc + 1) * 8],
                                    scalar1=TK[:, tc * 8 + 3:tc * 8 + 4],
                                    scalar2=TK[:, tc * 8 + 7:tc * 8 + 8],
                                    op0=ALU.is_equal, op1=ALU.mult)
                for tc in range(OWN):
                    ad = v.tensor_add(self.COMB[:, tc * 8:(tc + 1) * 8],
                                      self.TKC1[:, tc * 8:(tc + 1) * 8],
                                      self.TKE[:, tc * 8:(tc + 1) * 8])
                    if tc == OWN - 1:
                        ad.then_inc(self.sV, 1)
            blk.vector(_v11b)
            vC = v_sco + 1
            v_comb = vC
            self.dump(blk, "d_comb", self.COMB[:], self.sV, v_comb)

            t12, v12, s12, qg12, qs12 = tC, vC, sC, qg, qs
            st12 = self.stC
            n_k, n_tc = 16, 8
            bank = lambda par, j: (self.pgen[j] if par == 0 else
                                   (self.pgen[4 + j] if j < 2 else self.pbank[j - 2]))

            t_marks = {}
            tt = t12
            for tb in range(NTB):
                for e in range(E):
                    for k in range(n_k):
                        tt += 1; t_marks[("gu", tb, e, k)] = tt
                    for sweep in range(4):
                        for ic in range(16):
                            tt += 1; t_marks[("w2", tb, e, sweep, ic)] = tt
            g_marks = {}
            gg = qg12
            for tb in range(NTB):
                for e in range(E):
                    for k in range(n_k):
                        gg += 16; g_marks[("w13g", tb, e, k)] = gg
                        gg += 16; g_marks[("w13u", tb, e, k)] = gg
                    for ic in range(16):
                        gg += 16; g_marks[("w2", tb, e, ic)] = gg
            vseq = []
            for tb in range(NTB):
                for e in range(E):
                    for k in range(n_k):
                        vseq.append(("act", tb, e, k))
                        if e == 0 and tb >= 1:
                            if k < n_tc:
                                vseq.append(("zst", tb - 1, k))
                            elif k == n_tc:
                                vseq.append(("var3", tb - 1, None))
                            else:
                                vseq.append(("norm", tb - 1, k - n_tc - 1))
                    if e == 0 and tb >= 1:
                        vseq.append(("norm", tb - 1, n_tc - 1))
                    for sweep in range(4):
                        vseq.append(("o2", tb, e, sweep))
                        if tb == NTB - 1 and e == E - 1:
                            vseq.append(("zst", tb, sweep * 2))
                            vseq.append(("zst", tb, sweep * 2 + 1))
            tbl = NTB - 1
            vseq.append(("var3", tbl, None))
            for tc in range(n_tc):
                vseq.append(("norm", tbl, tc))
            v_marks = {}
            vv2 = v12
            for op in vseq:
                vv2 += 1
                if op[0] == "act":
                    v_marks[("act", op[1], op[2], op[3])] = vv2
                elif op[0] == "o2":
                    v_marks[("o2", op[1], op[2], op[3])] = vv2
                elif op[0] == "var3":
                    v_marks[("var3", op[1])] = vv2
                elif op[0] == "norm":
                    v_marks[("norm", op[1], op[2])] = vv2
            s_marks = {}
            ss2 = s12
            for tb in range(NTB):
                for e in range(E):
                    for k in range(n_k):
                        ss2 += 1; s_marks[("silu", tb, e, k)] = ss2
                        if tb >= 1 and e == 0 and k == n_tc:
                            ss2 += 1; s_marks[("rs", tb - 1, 0)] = ss2
            ss2 += 1
            s_marks[("rs", tbl, 0)] = ss2
            st_marks = {}
            st_ = st12
            for tb in range(NTB):
                for tc in range(n_tc):
                    st_ += 16; st_marks[(tb, tc)] = st_

            def _g12(g):
                g.wait_ge(self.sV, v_y1)
                g.wait_ge(self.sT, t10 + 2 + OWN)
                w13cnt = 0
                for tb in range(NTB):
                    for e in range(E):
                        for k in range(n_k):
                            for half in range(2):
                                idx = w13cnt % 4
                                if w13cnt >= 4:
                                    kk = w13cnt - 4
                                    tbp, rem = divmod(kk, E * 2 * n_k)
                                    ep, rem2 = divmod(rem, 2 * n_k)
                                    kp = rem2 // 2
                                    g.wait_ge(self.sT, t_marks[("gu", tbp, ep, kp)])
                                m = k if half == 0 else 16 + k
                                g.dma_start(self.W13B[idx][:], self.w13b[e, m, :, :]
                                            ).then_inc(self.sQG, 16)
                                w13cnt += 1
                        if (tb, e) != (0, 0):
                            pe = tb * E + e - 1
                            ptb, pex = divmod(pe, E)
                            g.wait_ge(self.sT, t_marks[("w2", ptb, pex, 3, 15)])
                        for ic in range(16):
                            g.dma_start(self.W2ALL[:, ic * 1024:(ic + 1) * 1024],
                                        self.w2t[e, ic * 128:(ic + 1) * 128, :]
                                        ).then_inc(self.sQG, 16)
            blk.gpsimd(_g12)
            qg = gg

            def _t12(t):
                t.wait_ge(self.sV, vcast + 132 + OWN)
                for tb in range(NTB):
                    for e in range(E):
                        for k in range(n_k):
                            par = k % 2
                            if k >= 2:
                                t.wait_ge(self.sV, v_marks[("act", tb, e, k - 2)])
                            elif (tb, e) != (0, 0):
                                pe = tb * E + e - 1
                                ptb, pex = divmod(pe, E)
                                t.wait_ge(self.sV, v_marks[("o2", ptb, pex, 3)])
                            for half in range(2):
                                gm = g_marks[("w13g" if half == 0 else "w13u", tb, e, k)]
                                t.wait_ge(self.sQG, gm)
                                blkidx = ((tb * E + e) * n_k + k) * 2 + half
                                for ts in range(2):
                                    p = bank(par, 2 * half + ts)
                                    for dcx in range(8):
                                        lhs = self.W13B[blkidx % 4][:, dcx * 128:(dcx + 1) * 128]
                                        rhs = self.Y1FB[:, dcx * TOK + tb * TB + ts * 512:
                                                        dcx * TOK + tb * TB + ts * 512 + 512]
                                        mm = t.matmul(p[:], lhs, rhs,
                                                      start=(dcx == 0), stop=(dcx == 7))
                                        if half == 1 and ts == 1 and dcx == 7:
                                            mm.then_inc(self.sT, 1)
                        for sweep in range(4):
                            par = sweep % 2
                            if sweep == 0:
                                t.wait_ge(self.sV, v_marks[("act", tb, e, n_k - 2)])
                            elif sweep == 1:
                                t.wait_ge(self.sV, v_marks[("act", tb, e, n_k - 1)])
                            else:
                                t.wait_ge(self.sV, v_marks[("o2", tb, e, sweep - 2)])
                            for ic in range(16):
                                t.wait_ge(self.sQG, g_marks[("w2", tb, e, ic)])
                                for tci in range(2):
                                    tc = sweep * 2 + tci
                                    for ns in range(2):
                                        p = bank(par, 2 * tci + ns)
                                        lhs = self.ACT[:, ic * TB + tc * 128:
                                                       ic * TB + tc * 128 + 128]
                                        rhs = self.W2ALL[:, ic * 1024 + ns * 512:
                                                         ic * 1024 + (ns + 1) * 512 if ns else
                                                         ic * 1024 + 512]
                                        rhs = self.W2ALL[:, ic * 1024 + ns * 512:
                                                         ic * 1024 + ns * 512 + 512]
                                        mm = t.matmul(p[:], lhs, rhs,
                                                      start=(ic == 0), stop=(ic == 15))
                                        if tci == 1 and ns == 1:
                                            mm.then_inc(self.sT, 1)
            blk.tensor(_t12)
            tC = tt

            def _v12(v):
                if DEBUG:
                    v.wait_ge(self.sST, self.stC)
                fin = self.OUT2[E % 2]
                zb = self.OUT2[1 - (E % 2)]
                for op in vseq:
                    if op[0] == "act":
                        _, tb, e, k = op
                        par = k % 2
                        v.wait_ge(self.sS, s_marks[("silu", tb, e, k)])
                        for ts in range(2):
                            pup = bank(par, 2 + ts)
                            mu = v.tensor_mul(
                                self.ACT[:, k * TB + ts * 512: k * TB + ts * 512 + 512],
                                self.SG[:, par * 1024 + ts * 512: par * 1024 + (ts + 1) * 512],
                                pup[:])
                            if ts == 1:
                                mu.then_inc(self.sV, 1)
                    elif op[0] == "o2":
                        _, tb, e, sweep = op
                        par = sweep % 2
                        v.wait_ge(self.sT, t_marks[("w2", tb, e, sweep, 15)])
                        for tci in range(2):
                            tc = sweep * 2 + tci
                            ccol = self.COMB[:, (tb * 8 + tc) * 8 + e:
                                             (tb * 8 + tc) * 8 + e + 1]
                            for ns in range(2):
                                p = bank(par, 2 * tci + ns)
                                dst = self.OUT2[(e + 1) % 2][:, tc * D + ns * 512:
                                                             tc * D + ns * 512 + 512]
                                if e == 0:
                                    o = v.tensor_scalar(out=dst, in0=p[:], scalar1=ccol,
                                                        scalar2=None, op0=ALU.mult)
                                else:
                                    o = v.scalar_tensor_tensor(
                                        out=dst, in0=p[:], scalar=ccol,
                                        in1=self.OUT2[e % 2][:, tc * D + ns * 512:
                                                             tc * D + ns * 512 + 512],
                                        op0=ALU.mult, op1=ALU.add)
                                if tci == 1 and ns == 1:
                                    o.then_inc(self.sV, 1)
                    elif op[0] == "zst":
                        _, tbn, tc = op
                        v.tensor_add(zb[:, tc * D:(tc + 1) * D],
                                     self.Y1TB[:, (tbn * 8 + tc) * D:(tbn * 8 + tc + 1) * D],
                                     fin[:, tc * D:(tc + 1) * D])
                        v.reduce_sum(ST[:, tc:tc + 1], zb[:, tc * D:(tc + 1) * D], axis=AX.X)
                        v.scalar_tensor_tensor(
                            out=self.ln3scr[:], in0=zb[:, tc * D:(tc + 1) * D], scalar=0.0,
                            in1=zb[:, tc * D:(tc + 1) * D], op0=ALU.add, op1=ALU.mult,
                            accum_out=ST[:, 8 + tc:9 + tc]).then_inc(self.sV, 1)
                    elif op[0] == "var3":
                        self.vsync(v, v.tensor_scalar_mul(ST[:, 16:24], ST[:, 0:8], -1.0 / D))
                        v.tensor_mul(ST[:, 32:40], ST[:, 16:24], ST[:, 16:24])
                        self.vsync(v, v.tensor_scalar_mul(ST[:, 40:48], ST[:, 8:16], 1.0 / D))
                        v.tensor_sub(ST[:, 8:16], ST[:, 40:48], ST[:, 32:40]
                                     ).then_inc(self.sV, 1)
                    else:
                        _, tbn, tc = op
                        if tc == 0:
                            v.wait_ge(self.sS, s_marks[("rs", tbn, 0)])
                        if tbn > 0 or tc >= 2:
                            ptb, ptc = (tbn, tc - 2) if tc >= 2 else (tbn - 1, n_tc + tc - 2)
                            v.wait_ge(self.sST, st_marks[(ptb, ptc)])
                        v.tensor_scalar(
                            out=self.OUTB[tc % 2][:], in0=zb[:, tc * D:(tc + 1) * D],
                            scalar1=ST[:, 16 + tc:17 + tc], scalar2=ST[:, 24 + tc:25 + tc],
                            op0=ALU.add, op1=ALU.mult,
                        ).then_inc(self.sV, 1)
            blk.vector(_v12)
            vC = vv2

            def _s12(s):
                def rs_pair(tbn):
                    s.wait_ge(self.sV, v_marks[("var3", tbn)])
                    self.ssync(s, s.activation(ST[:, 40:48], ST[:, 8:16],
                                               AF.Ln, bias=self.epst[:], scale=1.0))
                    s.activation(ST[:, 24:32], ST[:, 40:48], AF.Exp,
                                 scale=-0.5).then_inc(self.sS, 1)
                for tb in range(NTB):
                    for e in range(E):
                        for k in range(n_k):
                            par = k % 2
                            s.wait_ge(self.sT, t_marks[("gu", tb, e, k)])
                            for ts in range(2):
                                pg = bank(par, 0 + ts)
                                ac = s.activation(
                                    self.SG[:, par * 1024 + ts * 512: par * 1024 + (ts + 1) * 512],
                                    pg[:], AF.Silu)
                                if ts == 1:
                                    ac.then_inc(self.sS, 1)
                            if tb >= 1 and e == 0 and k == n_tc:
                                rs_pair(tb - 1)
                rs_pair(tbl)
            blk.scalar(_s12)
            sC = ss2

            def _sy12(sy):
                for tb in range(NTB):
                    for tc in range(n_tc):
                        sy.wait_ge(self.sV, v_marks[("norm", tb, tc)])
                        row = (tb * 8 + tc) * 128
                        sy.dma_start(self.out[row:row + 128, :], self.OUTB[tc % 2][:]
                                     ).then_inc(self.sST, 16)
                sy.wait_ge(self.sST, st12 + NTB * n_tc * 16)
            blk.sync(_sy12)
            self.stC = st12 + NTB * n_tc * 16
        return self.nc


def _stage_inputs(L0_attn_x, L0_attn_norm_w, L0_attn_norm_b, L0_attn_qkv_w,
                  L0_attn_qkv_b, L0_attn_proj_w, L0_attn_proj_b, L0_ffn_gate_w,
                  L0_ffn_w13_weight, L0_ffn_w2_weight):
    x = np.asarray(L0_attn_x, dtype=np.float32)
    qkvT = np.ascontiguousarray(np.asarray(L0_attn_qkv_w, np.float32).T)
    qkvb = np.ascontiguousarray(np.asarray(L0_attn_qkv_b, np.float32).reshape(1, 3 * D))
    nw = np.ascontiguousarray(np.asarray(L0_attn_norm_w, np.float32).reshape(8, 128).T)
    nb = np.ascontiguousarray(np.asarray(L0_attn_norm_b, np.float32).reshape(8, 128).T)
    projT = np.ascontiguousarray(np.asarray(L0_attn_proj_w, np.float32).T)
    projb = np.ascontiguousarray(np.asarray(L0_attn_proj_b, np.float32).reshape(1, D))
    gateT = np.ascontiguousarray(np.asarray(L0_ffn_gate_w, np.float32).T)
    w13 = np.asarray(L0_ffn_w13_weight, np.float32)
    w13b = np.ascontiguousarray(
        w13.transpose(0, 2, 1).reshape(E, 8, 128, 32, 128).transpose(0, 3, 2, 1, 4)
        .reshape(E, 32, 128, 1024))
    w2t = np.ascontiguousarray(np.asarray(L0_ffn_w2_weight, np.float32).transpose(0, 2, 1))
    eye = np.eye(128, dtype=np.float32)
    shared = dict(qkvT=qkvT, qkvb=qkvb, nw=nw, nb=nb, projT=projT, projb=projb,
                  gateT=gateT, w13b=w13b, w2t=w2t, eye=eye)
    in_maps = []
    for c in range(NCORES):
        n, q = divmod(c, 4)
        xr = np.ascontiguousarray(np.roll(x[n], -q * TOK, axis=0))
        m = dict(shared)
        m["xfull"] = xr
        in_maps.append(m)
    return in_maps


def kernel(**inputs):
    in_maps = _stage_inputs(**inputs)
    mk = MK()
    nc = mk.build()
    res = run_bass_kernel_spmd(nc, in_maps, core_ids=list(range(NCORES)))
    out = np.empty((B, L, D), dtype=np.float32)
    for c in range(NCORES):
        n, q = divmod(c, 4)
        out[n, q * TOK:(q + 1) * TOK, :] = res.results[c]["out"]
    return out


# version 17
# speedup vs baseline: 1.1032x; 1.1032x over previous
import numpy as np
import concourse.bass as bass
import concourse.mybir as mybir
from concourse.bass_utils import run_bass_kernel_spmd

F32 = mybir.dt.float32
BF16 = mybir.dt.bfloat16
AF = mybir.ActivationFunctionType
ALU = mybir.AluOpType
AX = mybir.AxisListType

DEBUG = False

B, L, D = 2, 8192, 1024
NCORES = 8
TOK = 2048
CH = 64
OWN = 16
DC = 8
DEW = 1032
DECN = 9
E = 8
INTER = 2048
TB = 1024
NTB = 2
SCALE = float(np.float32(1.0 / np.sqrt(L)))
EPS = 1e-5
PADCYC = 384


class MK:
    def __init__(self):
        nc = bass.Bass("TRN2", target_bir_lowering=False, detect_race_conditions=False)
        self.nc = nc
        d = nc.dram_tensor
        self.xfull = d("xfull", [L, D], F32, kind="ExternalInput").ap()
        self.qkvT = d("qkvT", [D, 3 * D], F32, kind="ExternalInput").ap()
        self.qkvb = d("qkvb", [1, 3 * D], F32, kind="ExternalInput").ap()
        self.nw = d("nw", [128, 8], F32, kind="ExternalInput").ap()
        self.nb = d("nb", [128, 8], F32, kind="ExternalInput").ap()
        self.projT = d("projT", [D, D], F32, kind="ExternalInput").ap()
        self.projb = d("projb", [1, D], F32, kind="ExternalInput").ap()
        self.gateT = d("gateT", [D, E], F32, kind="ExternalInput").ap()
        self.w13b = d("w13b", [E, 32, 128, 1024], F32, kind="ExternalInput").ap()
        self.w2t = d("w2t", [E, INTER, D], F32, kind="ExternalInput").ap()
        self.eye = d("eye", [128, 128], F32, kind="ExternalInput").ap()
        self.out = d("out", [TOK, D], F32, kind="ExternalOutput").ap()
        self.dbg = {}
        if DEBUG:
            for name, shape, dt in [
                ("d_ht", [128, CH * DEW], BF16), ("d_gs", [128, DECN * DEW], BF16),
                ("d_m1t", [128, DECN * D], BF16), ("d_wmat", [128, 8 * D], BF16),
                ("d_rsw", [128, 8], F32), ("d_vf", [128, 8 * TOK], BF16),
                ("d_at", [128, 8 * TOK], BF16), ("d_y1", [128, OWN * D], F32),
                ("d_lgt", [8, TOK], F32), ("d_comb", [128, OWN * E], F32),
                ("d_wqx", [128, DECN * D], BF16), ("d_wvx", [128, DECN * D], BF16),
                ("d_hof", [128, DECN * TOK], BF16),
            ]:
                self.dbg[name] = d(name, shape, dt, kind="ExternalOutput").ap()

    def alloc(self, ctx):
        nc = self.nc
        ent = ctx.enter_context
        self.eyeb = ent(nc.sbuf_tensor("eyeb", [128, 128], BF16))
        self.eyef = ent(nc.sbuf_tensor("eyef", [128, 128], F32))
        self.nwc = ent(nc.sbuf_tensor("nwc", [128, 8], F32))
        self.nbc = ent(nc.sbuf_tensor("nbc", [128, 8], F32))
        self.epst = ent(nc.sbuf_tensor("epst", [128, 1], F32))
        self.stats = ent(nc.sbuf_tensor("stats", [128, 256], F32))
        self.rsw = ent(nc.sbuf_tensor("rsw", [128, 8], F32))
        self.gatets = ent(nc.sbuf_tensor("gatets", [128, 64], F32))
        self.onesf = ent(nc.sbuf_tensor("onesf", [1, 128], F32))

        SLAB = 193536
        self.slab = ent(nc.sbuf_tensor("slab", [128, SLAB], mybir.dt.uint8))
        base = nc.lookup_mloc(self.slab).addr
        self._ranges = []

        def ov(name, off, shape, dtype):
            per = int(np.prod(shape[1:])) * mybir.dt.size(dtype)
            assert off % 32 == 0, name
            assert off + per <= SLAB, (name, off, per)
            self._ranges.append((name, off, off + per))
            hnd = nc.virtual_sbuf_tensor(name, shape, dtype)
            nc._tensor(hnd.name, hnd.shape, hnd.dtype, type="SB")
            mloc = nc.lookup_mloc(hnd)
            mloc.allocated = True
            mloc.addr = base + off
            return hnd

        self.HT = ov("HT", 0, [128, CH * DEW], BF16)
        self.GS = ov("GS", 132096, [128, DECN * DEW], BF16)
        self.HOF = ov("HOF", 156672, [128, DECN * TOK], BF16)
        self.xbuf = [ov(f"xbuf{i}", 132096 + i * 4096, [128, D], F32) for i in range(6)]
        self.WQX = ov("WQX", 0, [128, DECN * D], BF16)
        self.WKX = ov("WKX", 18432, [128, DECN * D], BF16)
        self.WVX = ov("WVX", 36864, [128, DECN * D], BF16)
        self.M1T = ov("M1T", 55296, [128, DECN * D], BF16)
        self.WMAT = ov("WMAT", 73728, [128, 8 * D], BF16)
        self.WT = ov("WT", 90112, [128, 8 * D], BF16)
        self.qbuf = [ov(f"qbuf{i}", 106496 + i * 12288, [128, 3 * D], F32) for i in range(2)]
        self.brows = ent(nc.sbuf_tensor("brows", [1, 4 * D], F32))
        self.qkvbS = self.brows[0:1, 0:3 * D]
        self.projbR = self.brows[0:1, 3 * D:4 * D]
        self.VF = ov("VF", 0, [128, 8 * TOK], BF16)
        self.AT = ov("AT", 132096, [128, 8 * TOK], BF16)
        self.XOWN = ov("XOWN", 0, [128, OWN * D], F32)
        self.Y1TOKF = ov("Y1TOKF", 65536, [128, OWN * D], F32)
        self.PJ = ov("PJ", 164864, [128, 8 * D], BF16)
        self.projbB = ov("projbB", 181248, [128, D], F32)
        self.z1pre = ov("z1pre", 189440, [128, D], F32)
        self.z1b = ov("z1b", 185344, [128, D], F32)
        self.Y1TB = ov("Y1TB", 0, [128, OWN * D], BF16)
        self.Y1FB = ov("Y1FB", 32768, [128, 8 * TOK], BF16)
        self.LGT = ov("LGT", 131072, [8, TOK], F32)
        self.strip = [ov(f"strip{i}", 139264 + i * 2048, [128, 512], F32) for i in range(2)]
        self.SCO = ov("SCO", 143360, [128, OWN * E], F32)
        self.EXPS = ov("EXPS", 143872, [128, OWN * E], F32)
        self.TKG = ov("TKG", 144384, [128, OWN * 4], F32)
        self.TKG2 = ov("TKG2", 144640, [128, OWN * 4], F32)
        self.TKM = ov("TKM", 144896, [128, OWN * E], F32)
        self.TKM2 = ov("TKM2", 145408, [128, OWN * E], F32)
        self.TKE = ov("TKE", 145920, [128, OWN * E], F32)
        self.TKC1 = ov("TKC1", 146432, [128, OWN * E], F32)
        self.TKS = ov("TKS", 146944, [128, OWN * 8], F32)
        self.COMB = ov("COMB", 147456, [128, OWN * E], F32)
        self.ACT = ov("ACT", 65536, [128, 16 * TB], BF16)
        self.OUT2 = [ov("OUT2a", 98304, [128, 8 * D], BF16),
                     ov("OUT2b", 114688, [128, 8 * D], BF16)]
        self.W2ALL = ov("W2ALL", 148480, [128, 16 * 1024], BF16)
        self.W13B = [ov(f"W13B{i}", 181248 + i * 2048, [128, 1024], BF16) for i in range(4)]
        self.SG = ov("SG", 189440, [128, 2048], BF16)
        self.Z2 = [ov("Z2x", 143360, [128, D], F32)]
        self.ln3scr = ov("ln3scr", 139264, [128, D], F32)
        self.OUTB = [ov(f"OUTB{i}", 131072 + i * 4096, [128, D], F32) for i in range(2)]

        self.pgen = [ent(nc.psum_tensor(f"pg{i}", [128, 512], F32)) for i in range(6)]
        self.pbank = [ent(nc.psum_tensor(f"pb{i}", [128, 512], F32)) for i in range(2)]

        def pov(name, i, shape, dtype):
            h = nc.virtual_psum_tensor(name, shape, dtype)
            nc._tensor(h.name, h.shape, h.dtype, type="PSUM")
            m = nc.lookup_mloc(h)
            m.allocated = True
            m.bank = nc.lookup_mloc(self.pbank[i]).bank
            m.addr = 0
            return h

        self.pt = [pov(f"ptx{i}", i, [128, 128], BF16) for i in range(2)]
        self.ptf = [pov(f"ptfx{i}", i, [128, 128], F32) for i in range(2)]

        def pov2(name, src_t, shape, dtype):
            h = nc.virtual_psum_tensor(name, shape, dtype)
            nc._tensor(h.name, h.shape, h.dtype, type="PSUM")
            m = nc.lookup_mloc(h)
            m.allocated = True
            m.bank = nc.lookup_mloc(src_t).bank
            m.addr = 0
            return h

        self.pgc = [pov2(f"pgcx{i}", self.pgen[4 + i], [128, 16], F32) for i in range(2)]
        sem = lambda n: ent(nc.semaphore(n))
        self.sT, self.sV, self.sS = sem("sT"), sem("sV"), sem("sS")
        self.sQS, self.sQG, self.sST = sem("sQS"), sem("sQG"), sem("sST")
        self.sVx, self.sSx = sem("sVx"), sem("sSx")
        self.vx_cnt, self.sx_cnt = 0, 0

    def vsync(self, v, inst):
        self.vx_cnt += 1
        inst.then_inc(self.sVx, 1)
        v.wait_ge(self.sVx, self.vx_cnt)

    def ssync(self, s, inst):
        self.sx_cnt += 1
        inst.then_inc(self.sSx, 1)
        s.wait_ge(self.sSx, self.sx_cnt)

    def dump(self, blk, name, src_ap, wait_sem, wait_val):
        if not DEBUG or name not in self.dbg:
            return
        dst = self.dbg[name]
        st0 = self.stC

        def _sy(sy):
            sy.wait_ge(wait_sem, wait_val)
            sy.dma_start(dst, src_ap).then_inc(self.sST, 16)
            sy.wait_ge(self.sST, st0 + 16)
        blk.sync(_sy)
        self.stC = st0 + 16

    def build(self):
        from contextlib import ExitStack
        with ExitStack() as ctx:
            self.alloc(ctx)
            blk = ctx.enter_context(self.nc.Block())
            self.stC = 0

            qs = 0

            def _sy0(sy):
                sy.dma_start(self.eyef[:], self.eye).then_inc(self.sQS, 16)
                sy.dma_start(self.nwc[:], self.nw).then_inc(self.sQS, 16)
                sy.dma_start(self.nbc[:], self.nb).then_inc(self.sQS, 16)
                src = self.gateT.rearrange("(c p) e -> p c e", p=128)
                dst = self.gatets.rearrange("p (c e) -> p c e", c=8)
                sy.dma_start(dst, src).then_inc(self.sQS, 16)
                sy.dma_start(self.qkvbS, self.qkvb).then_inc(self.sQS, 16)
                sy.dma_start(self.projbR, self.projb).then_inc(self.sQS, 16)
            blk.sync(_sy0)
            qs += 6 * 16
            qg = 0

            def _g0(g):
                g.dma_start(self.eyeb[:], self.eye).then_inc(self.sQG, 16)
            blk.gpsimd(_g0)
            qg += 16

            vC = 0

            def _v0(v):
                v.memset(self.epst[:], EPS)
                v.memset(self.onesf[:], 1.0)
                v.memset(self.HT[:, 0:CH * DEW // 2], 0.0)
                v.memset(self.HT[:, CH * DEW // 2:], 0.0)
                ones_cols = self.HT.rearrange("p (c w) -> p c w", c=CH)[:, :, 1024:1025]
                v.memset(ones_cols, 1.0).then_inc(self.sV, 1)
            blk.vector(_v0)
            vC += 1

            qs1 = qs
            v1 = vC
            ST = self.stats
            lngr = [(g * 3, min(g * 3 + 3, CH)) for g in range((CH + 2) // 3)]
            NG = len(lngr)
            vA, vB, v_norm = {}, {}, {}
            vv = v1
            for gi, (a, b) in enumerate(lngr):
                if gi >= 2:
                    for cc in range(*lngr[gi - 2]):
                        vv += 1; v_norm[cc] = vv
                for cc in range(a, b):
                    vv += 1; vA[cc] = vv
                if gi >= 1:
                    vv += 1; vB[gi - 1] = vv
            vv += 1; vB[NG - 1] = vv
            for gg in (NG - 2, NG - 1):
                for cc in range(*lngr[gg]):
                    vv += 1; v_norm[cc] = vv
            sA, sR = {}, {}
            ss = 0
            for gi, (a, b) in enumerate(lngr):
                for cc in range(a, b):
                    ss += 1; sA[cc] = ss
                if gi >= 1:
                    ss += 1; sR[gi - 1] = ss
            ss += 1; sR[NG - 1] = ss

            def _sy1(sy):
                for cc in range(CH):
                    if cc >= 6:
                        sy.wait_ge(self.sV, v_norm[cc - 6])
                    sy.dma_start(self.xbuf[cc % 6][:],
                                 self.xfull[cc * 128:(cc + 1) * 128, :]).then_inc(self.sQS, 16)
            blk.sync(_sy1)
            qs = qs1 + CH * 16

            def _norm1(v, gg):
                v.wait_ge(self.sS, sR[gg])
                for cc in range(*lngr[gg]):
                    v.tensor_scalar(
                        out=self.HT[:, cc * DEW:cc * DEW + D], in0=self.xbuf[cc % 6][:],
                        scalar1=ST[:, 128 + cc:129 + cc], scalar2=ST[:, 192 + cc:193 + cc],
                        op0=ALU.add, op1=ALU.mult,
                    ).then_inc(self.sV, 1)

            def _smalls1(v, gg):
                a, b = lngr[gg]
                v.wait_ge(self.sS, sA[b - 1])
                self.vsync(v, v.tensor_scalar_mul(ST[:, 128 + a:128 + b],
                                                  ST[:, a:b], -1.0 / D))
                v.tensor_mul(ST[:, a:b], ST[:, 128 + a:128 + b], ST[:, 128 + a:128 + b])
                self.vsync(v, v.tensor_scalar_mul(ST[:, 192 + a:192 + b],
                                                  ST[:, 64 + a:64 + b], 1.0 / D))
                v.tensor_sub(ST[:, 64 + a:64 + b], ST[:, 192 + a:192 + b],
                             ST[:, a:b]).then_inc(self.sV, 1)

            def _v1(v):
                for gi, (a, b) in enumerate(lngr):
                    if gi >= 2:
                        _norm1(v, gi - 2)
                    for cc in range(a, b):
                        v.wait_ge(self.sQS, qs1 + (cc + 1) * 16)
                        v.reduce_sum(ST[:, cc:cc + 1], self.xbuf[cc % 6][:], axis=AX.X
                                     ).then_inc(self.sV, 1)
                    if gi >= 1:
                        _smalls1(v, gi - 1)
                _smalls1(v, NG - 1)
                _norm1(v, NG - 2)
                _norm1(v, NG - 1)
            blk.vector(_v1)
            vC = vv

            def _s1(s):
                for gi, (a, b) in enumerate(lngr):
                    for cc in range(a, b):
                        s.wait_ge(self.sQS, qs1 + (cc + 1) * 16)
                        s.activation(self.HT[:, cc * DEW:cc * DEW + D], self.xbuf[cc % 6][:],
                                     AF.Square, accum_out=ST[:, 64 + cc:65 + cc]
                                     ).then_inc(self.sS, 1)
                    if gi >= 1:
                        s.wait_ge(self.sV, vB[gi - 1])
                        self.ssync(s, s.activation(ST[:, lngr[gi - 1][0]:lngr[gi - 1][1]],
                                                   ST[:, 64 + lngr[gi - 1][0]:64 + lngr[gi - 1][1]],
                                                   AF.Ln, bias=self.epst[:], scale=1.0))
                        s.activation(ST[:, 192 + lngr[gi - 1][0]:192 + lngr[gi - 1][1]],
                                     ST[:, lngr[gi - 1][0]:lngr[gi - 1][1]], AF.Exp, scale=-0.5
                                     ).then_inc(self.sS, 1)
                s.wait_ge(self.sV, vB[NG - 1])
                self.ssync(s, s.activation(ST[:, lngr[NG - 1][0]:lngr[NG - 1][1]],
                                           ST[:, 64 + lngr[NG - 1][0]:64 + lngr[NG - 1][1]],
                                           AF.Ln, bias=self.epst[:], scale=1.0))
                s.activation(ST[:, 192 + lngr[NG - 1][0]:192 + lngr[NG - 1][1]],
                             ST[:, lngr[NG - 1][0]:lngr[NG - 1][1]], AF.Exp, scale=-0.5
                             ).then_inc(self.sS, 1)
            blk.scalar(_s1)
            sC = ss
            tC = 0
            t3, v3 = tC, vC

            def _t3(t):
                for i in range(OWN * 8):
                    cc, dcx = divmod(i, 8)
                    t.wait_ge(self.sV, v_norm[cc])
                    if i >= 2:
                        t.wait_ge(self.sV, v3 + i - 1)
                    t.transpose(self.pt[i % 2][:],
                                self.HT[:, cc * DEW + dcx * 128: cc * DEW + dcx * 128 + 128],
                                self.eyeb[:]).then_inc(self.sT, 1)
            blk.tensor(_t3)
            tC = t3 + OWN * 8

            def _v3(v):
                for i in range(OWN * 8):
                    cc, dcx = divmod(i, 8)
                    v.wait_ge(self.sT, t3 + i + 1)
                    v.tensor_copy(
                        self.HOF[:, dcx * TOK + cc * 128: dcx * TOK + cc * 128 + 128],
                        self.pt[i % 2][:]).then_inc(self.sV, 1)
                v.memset(self.HOF[0:8, 8 * TOK: 9 * TOK], 0.0)
                v.memset(self.HOF[0:1, 8 * TOK: 9 * TOK], 1.0).then_inc(self.sV, 1)
            blk.vector(_v3)
            vC = v3 + OWN * 8 + 1
            v_hof = vC
            self.dump(blk, "d_hof", self.HOF[:], self.sV, v_hof)

            t2, v2 = tC, vC

            def _t2(t):
                t.wait_ge(self.sV, v_norm[CH - 1])
                for mc in range(DECN):
                    M = 128 if mc < 8 else 8
                    pa = self.pgen[2 * (mc % 2)]
                    pb = self.pgen[2 * (mc % 2) + 1]
                    pc_ = self.pgc[mc % 2]
                    if mc >= 2:
                        t.wait_ge(self.sV, v2 + mc - 1)
                    for cc in range(CH):
                        lhs = self.HT[:, cc * DEW + mc * 128: cc * DEW + mc * 128 + M]
                        st_, sp = (cc == 0), (cc == CH - 1)
                        t.matmul(pa[0:M, :], lhs, self.HT[:, cc * DEW: cc * DEW + 512],
                                 start=st_, stop=sp)
                        t.matmul(pb[0:M, :], lhs, self.HT[:, cc * DEW + 512: cc * DEW + 1024],
                                 start=st_, stop=sp)
                        mm = t.matmul(pc_[0:M, 0:8], lhs,
                                      self.HT[:, cc * DEW + 1024: cc * DEW + 1032],
                                      start=st_, stop=sp)
                        if sp:
                            mm.then_inc(self.sT, 1)
            blk.tensor(_t2)
            tC = t2 + DECN

            def _v2(v):
                for mc in range(DECN):
                    M = 128 if mc < 8 else 8
                    pa = self.pgen[2 * (mc % 2)]
                    pb = self.pgen[2 * (mc % 2) + 1]
                    pc_ = self.pgc[mc % 2]
                    v.wait_ge(self.sT, t2 + mc + 1)
                    o = mc * DEW
                    v.tensor_copy(self.GS[0:M, o:o + 512], pa[0:M, :])
                    v.tensor_copy(self.GS[0:M, o + 512:o + 1024], pb[0:M, :])
                    v.tensor_copy(self.GS[0:M, o + 1024:o + 1032], pc_[0:M, 0:8]
                                  ).then_inc(self.sV, 1)
            blk.vector(_v2)
            vC = v2 + DECN
            v_g = vC
            self.dump(blk, "d_gs", self.GS[:], self.sV, v_g)

            qs4, v4, t4 = qs, vC, tC

            def _sy4(sy):
                sy.wait_ge(self.sT, t2 + DECN)
                for dcx in range(DC):
                    if dcx >= 2:
                        sy.wait_ge(self.sV, v4 + dcx - 1)
                        sy.wait_ge(self.sT, t4 + dcx - 1)
                    sy.dma_start(self.qbuf[dcx % 2][:],
                                 self.qkvT[dcx * 128:(dcx + 1) * 128, :]).then_inc(self.sQS, 16)
            blk.sync(_sy4)
            qs = qs4 + DC * 16

            def _v4(v):
                v.wait_ge(self.sT, t2 + DECN)
                if DEBUG:
                    v.wait_ge(self.sST, self.stC)
                for dcx in range(DC):
                    qb = self.qbuf[dcx % 2]
                    v.wait_ge(self.sQS, qs4 + (dcx + 1) * 16)
                    g = self.nwc[:, dcx:dcx + 1]
                    v.tensor_scalar(out=self.WQX[:, dcx * D:(dcx + 1) * D], in0=qb[:, 0:D],
                                    scalar1=g, scalar2=None, op0=ALU.mult)
                    v.tensor_scalar(out=self.WKX[:, dcx * D:(dcx + 1) * D], in0=qb[:, D:2 * D],
                                    scalar1=g, scalar2=None, op0=ALU.mult)
                    v.tensor_scalar(out=self.WVX[:, dcx * D:(dcx + 1) * D],
                                    in0=qb[:, 2 * D:3 * D],
                                    scalar1=g, scalar2=None, op0=ALU.mult).then_inc(self.sV, 1)
            blk.vector(_v4)
            vC = v4 + DC

            def _t4(t):
                for dcx in range(DC):
                    t.wait_ge(self.sQS, qs4 + (dcx + 1) * 16)
                    for j in range(6):
                        mm = t.matmul(self.pgen[j][0:1, :], self.nbc[:, dcx:dcx + 1],
                                      self.qbuf[dcx % 2][:, j * 512:(j + 1) * 512],
                                      start=(dcx == 0), stop=(dcx == DC - 1))
                        if j == 5:
                            mm.then_inc(self.sT, 1)
            blk.tensor(_t4)
            tC = t4 + DC

            def _v4b(v):
                v.wait_ge(self.sT, t4 + DC)
                for w in (self.WQX, self.WKX, self.WVX):
                    v.memset(w[0:8, 8 * D: 9 * D], 0.0)
                for j, (w, boff) in enumerate([(self.WQX, 0), (self.WQX, 512),
                                               (self.WKX, 1024), (self.WKX, 1536),
                                               (self.WVX, 2048), (self.WVX, 2560)]):
                    col = boff % 1024
                    v.tensor_add(w[0:1, 8 * D + col: 8 * D + col + 512],
                                 self.pgen[j][0:1, :], self.brows[0:1, boff:boff + 512]
                                 ).then_inc(self.sV, 1 if j == 5 else 1)
            blk.vector(_v4b)
            vC += 6
            v_wx = vC
            self.dump(blk, "d_wqx", self.WQX[:], self.sV, v_wx)
            self.dump(blk, "d_wvx", self.WVX[:], self.sV, v_wx)

            t5, v5 = tC, vC

            def _t5(t):
                t.wait_ge(self.sV, v_wx)
                for rc in range(DECN):
                    Mr = 128 if rc < 8 else 8
                    if rc >= 2:
                        t.wait_ge(self.sV, v5 + rc - 1)
                    for ns in range(2):
                        p = self.pgen[2 * (rc % 2) + ns]
                        for dcx in range(DECN):
                            K = 128 if dcx < 8 else 8
                            lhs = self.GS[0:K, dcx * DEW + rc * 128: dcx * DEW + rc * 128 + Mr]
                            rhs = self.WQX[0:K, dcx * D + ns * 512: dcx * D + ns * 512 + 512]
                            mm = t.matmul(p[0:Mr, :], lhs, rhs,
                                          start=(dcx == 0), stop=(dcx == DECN - 1))
                            if ns == 1 and dcx == DECN - 1:
                                mm.then_inc(self.sT, 1)
            blk.tensor(_t5)
            tC = t5 + DECN

            def _v5(v):
                for rc in range(DECN):
                    Mr = 128 if rc < 8 else 8
                    v.wait_ge(self.sT, t5 + rc + 1)
                    for ns in range(2):
                        p = self.pgen[2 * (rc % 2) + ns]
                        cp = v.tensor_copy(
                            self.M1T[0:Mr, rc * D + ns * 512: rc * D + ns * 512 + 512],
                            p[0:Mr, :])
                        if ns == 1:
                            cp.then_inc(self.sV, 1)
            blk.vector(_v5)
            vC = v5 + DECN
            v_m1 = vC
            self.dump(blk, "d_m1t", self.M1T[:], self.sV, v_m1)

            t6, v6, s6 = tC, vC, sC

            def _t6(t):
                t.wait_ge(self.sV, v_m1)
                for ic in range(8):
                    if ic >= 2:
                        t.wait_ge(self.sS, s6 + ic - 1)
                    for ns in range(2):
                        p = self.pgen[2 * (ic % 2) + ns]
                        for dcx in range(DECN):
                            K = 128 if dcx < 8 else 8
                            lhs = self.M1T[0:K, dcx * D + ic * 128: dcx * D + ic * 128 + 128]
                            rhs = self.WKX[0:K, dcx * D + ns * 512: dcx * D + ns * 512 + 512]
                            mm = t.matmul(p[:], lhs, rhs,
                                          start=(dcx == 0), stop=(dcx == DECN - 1))
                            if ns == 1 and dcx == DECN - 1:
                                mm.then_inc(self.sT, 1)
            blk.tensor(_t6)
            tC = t6 + 8

            def _v6(v):
                for ic in range(8):
                    pa = self.pgen[2 * (ic % 2)]
                    pb = self.pgen[2 * (ic % 2) + 1]
                    v.wait_ge(self.sT, t6 + ic + 1)
                    v.reduce_max(ST[:, 20:21], pa[:], axis=AX.X, negate=True)
                    self.vsync(v, v.reduce_max(ST[:, 21:22], pb[:], axis=AX.X, negate=True))
                    self.vsync(v, v.tensor_tensor(ST[:, 22:23], ST[:, 20:21],
                                                  ST[:, 21:22], ALU.min))
                    v.tensor_scalar_mul(ST[:, 24:25], ST[:, 22:23], SCALE).then_inc(self.sV, 1)
                    v.wait_ge(self.sS, s6 + ic + 1)
                    self.vsync(v, v.reduce_sum(ST[:, 23:24],
                                               self.WMAT[:, ic * D:(ic + 1) * D], axis=AX.X))
                    v.reciprocal(self.rsw[:, ic:ic + 1], ST[:, 23:24]).then_inc(self.sV, 1)
            blk.vector(_v6)
            vC = v6 + 16

            def _s6(s):
                for ic in range(8):
                    pa = self.pgen[2 * (ic % 2)]
                    pb = self.pgen[2 * (ic % 2) + 1]
                    s.wait_ge(self.sV, v6 + 2 * ic + 1)
                    s.activation(self.WMAT[:, ic * D: ic * D + 512], pa[:], AF.Exp,
                                 bias=ST[:, 24:25], scale=SCALE)
                    s.activation(self.WMAT[:, ic * D + 512: ic * D + 1024], pb[:], AF.Exp,
                                 bias=ST[:, 24:25], scale=SCALE).then_inc(self.sS, 1)
            blk.scalar(_s6)
            sC = s6 + 8
            v_soft = vC
            self.dump(blk, "d_wmat", self.WMAT[:], self.sS, sC)
            self.dump(blk, "d_rsw", self.rsw[:], self.sV, v_soft)

            t7, v7 = tC, vC

            def _t7(t):
                t.wait_ge(self.sV, v_soft)
                for i in range(64):
                    ic, jc = divmod(i, 8)
                    if i >= 2:
                        t.wait_ge(self.sV, v7 + i - 1)
                    t.transpose(self.pt[i % 2][:],
                                self.WMAT[:, ic * D + jc * 128: ic * D + jc * 128 + 128],
                                self.eyeb[:]).then_inc(self.sT, 1)
            blk.tensor(_t7)
            tC = t7 + 64

            def _v7(v):
                for i in range(64):
                    ic, jc = divmod(i, 8)
                    v.wait_ge(self.sT, t7 + i + 1)
                    v.tensor_copy(self.WT[:, jc * D + ic * 128: jc * D + ic * 128 + 128],
                                  self.pt[i % 2][:]).then_inc(self.sV, 1)
            blk.vector(_v7)
            vC = v7 + 64
            v_wt = vC

            t8, v8 = tC, vC

            def _t8(t):
                t.wait_ge(self.sV, v_wt)
                for i in range(32):
                    vc_, cs = divmod(i, 4)
                    if i >= 2:
                        t.wait_ge(self.sV, v8 + i - 1)
                    p = self.pgen[4 + (i % 2)]
                    for dcx in range(DECN):
                        K = 128 if dcx < 8 else 8
                        lhs = self.WVX[0:K, dcx * D + vc_ * 128: dcx * D + vc_ * 128 + 128]
                        rhs = self.HOF[0:K, dcx * TOK + cs * 512: dcx * TOK + cs * 512 + 512]
                        mm = t.matmul(p[:], lhs, rhs, start=(dcx == 0), stop=(dcx == DECN - 1))
                        if dcx == DECN - 1:
                            mm.then_inc(self.sT, 1)
            blk.tensor(_t8)
            tC = t8 + 32

            def _v8(v):
                if DEBUG:
                    v.wait_ge(self.sST, self.stC)
                for i in range(32):
                    vc_, cs = divmod(i, 4)
                    v.wait_ge(self.sT, t8 + i + 1)
                    p = self.pgen[4 + (i % 2)]
                    v.tensor_copy(self.VF[:, vc_ * TOK + cs * 512: vc_ * TOK + cs * 512 + 512],
                                  p[:]).then_inc(self.sV, 1)
            blk.vector(_v8)
            vC = v8 + 32
            v_vf = vC
            self.dump(blk, "d_vf", self.VF[:], self.sV, v_vf)

            t9, v9 = tC, vC

            def _t9(t):
                t.wait_ge(self.sV, v_vf)
                if DEBUG:
                    t.wait_ge(self.sST, self.stC)
                for i in range(32):
                    ic, cs = divmod(i, 4)
                    if i >= 4:
                        t.wait_ge(self.sV, v9 + i - 3)
                    p = self.pgen[i % 4]
                    for jc in range(8):
                        lhs = self.WT[:, jc * D + ic * 128: jc * D + ic * 128 + 128]
                        rhs = self.VF[:, jc * TOK + cs * 512: jc * TOK + cs * 512 + 512]
                        mm = t.matmul(p[:], lhs, rhs, start=(jc == 0), stop=(jc == 7))
                        if jc == 7:
                            mm.then_inc(self.sT, 1)
            blk.tensor(_t9)
            tC = t9 + 32

            def _v9(v):
                if DEBUG:
                    v.wait_ge(self.sST, self.stC)
                for i in range(32):
                    ic, cs = divmod(i, 4)
                    v.wait_ge(self.sT, t9 + i + 1)
                    p = self.pgen[i % 4]
                    v.tensor_scalar(
                        out=self.AT[:, ic * TOK + cs * 512: ic * TOK + cs * 512 + 512],
                        in0=p[:], scalar1=self.rsw[:, ic:ic + 1], scalar2=None,
                        op0=ALU.mult).then_inc(self.sV, 1)
            blk.vector(_v9)
            vC = v9 + 32
            v_at = vC
            self.dump(blk, "d_at", self.AT[:], self.sV, v_at)

            t10, v10, s10, qs10, qg10 = tC, vC, sC, qs, qg

            def _sy10(sy):
                sy.wait_ge(self.sV, v_at)
                if DEBUG:
                    sy.wait_ge(self.sST, self.stC)
                for cc in range(OWN):
                    sy.dma_start(self.XOWN[:, cc * D:(cc + 1) * D],
                                 self.xfull[cc * 128:(cc + 1) * 128, :]).then_inc(self.sQS, 16)
            blk.sync(_sy10)
            qs = qs10 + OWN * 16

            def _g10(g):
                g.wait_ge(self.sV, v_vf)
                if DEBUG:
                    g.wait_ge(self.sST, self.stC)
                for qc in range(8):
                    g.dma_start(self.PJ[:, qc * D:(qc + 1) * D],
                                self.projT[qc * 128:(qc + 1) * 128, :]).then_inc(self.sQG, 16)
            blk.gpsimd(_g10)
            qg = qg10 + 8 * 16

            v_st10 = {cc: v10 + 1 + 2 * cc + 1 for cc in range(OWN)}
            v_nm10 = {cc: v10 + 1 + 2 * cc + 2 for cc in range(OWN)}
            s_e10 = {cc: s10 + cc + 1 for cc in range(OWN)}

            def _t10(t):
                t.wait_ge(self.sV, v_at)
                t.wait_ge(self.sQS, qs10 + 16)
                for ns in range(2):
                    mm = t.matmul(self.pgen[4 + ns][:], self.onesf[:],
                                  self.brows[0:1, 3 * D + ns * 512: 3 * D + (ns + 1) * 512],
                                  start=True, stop=True)
                    mm.then_inc(self.sT, 1)
                t.wait_ge(self.sQG, qg10 + 8 * 16)
                for cc in range(OWN):
                    if cc >= 2:
                        t.wait_ge(self.sV, v_st10[cc - 2])
                    for ns in range(2):
                        p = self.pgen[2 * (cc % 2) + ns]
                        for qc in range(8):
                            lhs = self.AT[:, qc * TOK + cc * 128: qc * TOK + cc * 128 + 128]
                            rhs = self.PJ[:, qc * D + ns * 512: qc * D + ns * 512 + 512]
                            mm = t.matmul(p[:], lhs, rhs, start=(qc == 0), stop=(qc == 7))
                            if ns == 1 and qc == 7:
                                mm.then_inc(self.sT, 1)
            blk.tensor(_t10)
            tC = t10 + 2 + OWN

            def _v10(v):
                v.wait_ge(self.sT, t10 + 2)
                v.tensor_copy(self.projbB[:, 0:512], self.pgen[4][:])
                v.tensor_copy(self.projbB[:, 512:1024], self.pgen[5][:]).then_inc(self.sV, 1)
                for cc in range(OWN):
                    r = cc % 4
                    v.wait_ge(self.sT, t10 + 2 + cc + 1)
                    v.wait_ge(self.sQS, qs10 + (cc + 1) * 16)
                    xo = self.XOWN[:, cc * D:(cc + 1) * D]
                    for ns in range(2):
                        p = self.pgen[2 * (cc % 2) + ns]
                        v.scalar_tensor_tensor(
                            out=self.z1pre[:, ns * 512:(ns + 1) * 512],
                            in0=xo[:, ns * 512:(ns + 1) * 512], scalar=2.0,
                            in1=p[:], op0=ALU.mult, op1=ALU.add)
                    v.tensor_add(self.z1b[:], self.z1pre[:], self.projbB[:])
                    self.vsync(v, v.reduce_sum(ST[:, 32 + r:33 + r], self.z1b[:], axis=AX.X))
                    self.vsync(v, v.tensor_scalar_mul(ST[:, 36 + r:37 + r], ST[:, 32 + r:33 + r], 1.0 / D))
                    v.scalar_tensor_tensor(
                        out=self.z1pre[:], in0=self.z1b[:], scalar=ST[:, 36 + r:37 + r],
                        in1=self.z1b[:], op0=ALU.subtract, op1=ALU.mult,
                        accum_out=ST[:, 40 + r:41 + r],
                    ).then_inc(self.sV, 1)
                    v.wait_ge(self.sS, s_e10[cc])
                    v.tensor_scalar(
                        out=self.Y1TOKF[:, cc * D:(cc + 1) * D], in0=self.z1b[:],
                        scalar1=ST[:, 36 + r:37 + r], scalar2=ST[:, 48 + r:49 + r],
                        op0=ALU.subtract, op1=ALU.mult,
                    ).then_inc(self.sV, 1)
            blk.vector(_v10)
            vC = v10 + 1 + 2 * OWN

            def _s10(s):
                for cc in range(OWN):
                    r = cc % 4
                    s.wait_ge(self.sV, v_st10[cc])
                    self.ssync(s, s.activation(ST[:, 44 + r:45 + r], ST[:, 40 + r:41 + r],
                                                AF.Ln, bias=self.epst[:], scale=1.0 / D))
                    s.activation(ST[:, 48 + r:49 + r], ST[:, 44 + r:45 + r], AF.Exp,
                                 scale=-0.5).then_inc(self.sS, 1)
            blk.scalar(_s10)
            sC = s10 + OWN
            v_y1 = vC
            self.dump(blk, "d_y1", self.Y1TOKF[:], self.sV, v_y1)

            t11, v11, s11 = tC, vC, sC
            tr_mark = lambda g, k: t11 + g * 5 + k + 1
            mm_mark = lambda g: t11 + g * 5 + 5
            lt_mark = lambda tc: t11 + 160 + tc + 1
            vcast = v11 + 1
            vcp_mark = lambda g, k: vcast + g * 4 + k + 1
            vlgt_mark = lambda i: vcast + 128 + i + 1
            vsco_mark = lambda tc: vcast + 132 + tc + 1

            def _t11(t):
                t.wait_ge(self.sV, v_y1)
                for g in range(32):
                    ts4, dcx = divmod(g, 8)
                    for k in range(4):
                        tc = ts4 * 4 + k
                        i = g * 4 + k
                        if i >= 2:
                            t.wait_ge(self.sV, vcp_mark(*divmod(i - 2, 4)))
                        t.transpose(self.ptf[i % 2][:],
                                    self.Y1TOKF[:, tc * D + dcx * 128: tc * D + dcx * 128 + 128],
                                    self.eyef[:]).then_inc(self.sT, 1)
                    t.wait_ge(self.sV, vcp_mark(g, 3))
                    t.matmul(self.pgen[2 + ts4][0:8, :], self.gatets[:, dcx * 8:(dcx + 1) * 8],
                             self.strip[dcx % 2][:], start=(dcx == 0), stop=(dcx == 7)
                             ).then_inc(self.sT, 1)
                for tc in range(OWN):
                    t.wait_ge(self.sV, vlgt_mark(3))
                    if tc >= 2:
                        t.wait_ge(self.sV, vsco_mark(tc - 2))
                    t.transpose(self.ptf[tc % 2][:, 0:8],
                                self.LGT[0:8, tc * 128:(tc + 1) * 128],
                                self.eyef[0:8, 0:8]).then_inc(self.sT, 1)
            blk.tensor(_t11)
            tC = t11 + 160 + OWN

            def _v11(v):
                if DEBUG:
                    v.wait_ge(self.sST, self.stC)
                v.wait_ge(self.sV, v_y1)
                v.tensor_copy(self.Y1TB[:], self.Y1TOKF[:]).then_inc(self.sV, 1)
                for g in range(32):
                    ts4, dcx = divmod(g, 8)
                    if g >= 2:
                        v.wait_ge(self.sT, mm_mark(g - 2))
                    for k in range(4):
                        tc = ts4 * 4 + k
                        v.wait_ge(self.sT, tr_mark(g, k))
                        p = self.ptf[(g * 4 + k) % 2]
                        v.tensor_copy(self.strip[dcx % 2][:, k * 128:(k + 1) * 128], p[:])
                        v.tensor_copy(
                            self.Y1FB[:, dcx * TOK + tc * 128: dcx * TOK + tc * 128 + 128],
                            p[:]).then_inc(self.sV, 1)
                for i in range(4):
                    v.wait_ge(self.sT, mm_mark(i * 8 + 7))
                    v.tensor_copy(self.LGT[0:8, i * 512:(i + 1) * 512],
                                  self.pgen[2 + i][0:8, :]).then_inc(self.sV, 1)
                for tc in range(OWN):
                    v.wait_ge(self.sT, lt_mark(tc))
                    v.tensor_copy(self.SCO[:, tc * 8:(tc + 1) * 8],
                                  self.ptf[tc % 2][:, 0:8]).then_inc(self.sV, 1)
            blk.vector(_v11)
            vC = vcast + 128 + 4 + OWN
            v_sco = vC
            self.dump(blk, "d_lgt", self.LGT[:], self.sV, vcast + 132)

            def _s11(s):
                s.wait_ge(self.sV, v_sco)
                s.activation(self.EXPS[:], self.SCO[:], AF.Exp).then_inc(self.sS, 1)
            blk.scalar(_s11)
            sC = s11 + 1

            TK = self.TKS

            def _v11b(v):
                v.wait_ge(self.sS, s11 + 1)
                for tc in range(OWN):
                    sce = self.EXPS[:, tc * 8:(tc + 1) * 8]
                    v.tensor_max(self.TKG[:, tc * 4:(tc + 1) * 4], sce[:, 0:8:2], sce[:, 1:8:2])
                for tc in range(OWN):
                    v.reduce_max(TK[:, tc * 8:tc * 8 + 1], self.TKG[:, tc * 4:(tc + 1) * 4],
                                 axis=AX.X)
                for tc in range(OWN):
                    ge = self.TKG[:, tc * 4:(tc + 1) * 4]
                    v.scalar_tensor_tensor(out=self.TKG2[:, tc * 4:(tc + 1) * 4], in0=ge,
                                           scalar=TK[:, tc * 8:tc * 8 + 1], in1=ge,
                                           op0=ALU.is_lt, op1=ALU.mult)
                for tc in range(OWN):
                    v.reduce_max(TK[:, tc * 8 + 1:tc * 8 + 2],
                                 self.TKG2[:, tc * 4:(tc + 1) * 4], axis=AX.X)
                for tc in range(OWN):
                    v.tensor_scalar(out=self.TKG2[:, tc * 4:(tc + 1) * 4],
                                    in0=self.TKG[:, tc * 4:(tc + 1) * 4],
                                    scalar1=TK[:, tc * 8 + 1:tc * 8 + 2], scalar2=None,
                                    op0=ALU.is_ge)
                for tc in range(OWN):
                    em = self.TKE[:, tc * 8:(tc + 1) * 8]
                    gm = self.TKG2[:, tc * 4:(tc + 1) * 4]
                    v.tensor_copy(em[:, 0:8:2], gm)
                    v.tensor_copy(em[:, 1:8:2], gm)
                for tc in range(OWN):
                    v.tensor_mul(self.TKM[:, tc * 8:(tc + 1) * 8],
                                 self.EXPS[:, tc * 8:(tc + 1) * 8],
                                 self.TKE[:, tc * 8:(tc + 1) * 8])
                for tc in range(OWN):
                    v.reduce_max(TK[:, tc * 8 + 2:tc * 8 + 3],
                                 self.TKM[:, tc * 8:(tc + 1) * 8], axis=AX.X)
                for tc in range(OWN):
                    mk = self.TKM[:, tc * 8:(tc + 1) * 8]
                    v.scalar_tensor_tensor(out=self.TKM2[:, tc * 8:(tc + 1) * 8], in0=mk,
                                           scalar=TK[:, tc * 8 + 2:tc * 8 + 3], in1=mk,
                                           op0=ALU.is_lt, op1=ALU.mult)
                for tc in range(OWN):
                    v.reduce_max(TK[:, tc * 8 + 3:tc * 8 + 4],
                                 self.TKM2[:, tc * 8:(tc + 1) * 8], axis=AX.X)
                for tc in range(OWN):
                    v.tensor_add(TK[:, tc * 8 + 4:tc * 8 + 5], TK[:, tc * 8 + 2:tc * 8 + 3],
                                 TK[:, tc * 8 + 3:tc * 8 + 4])
                for tc in range(OWN):
                    v.reciprocal(TK[:, tc * 8 + 5:tc * 8 + 6], TK[:, tc * 8 + 4:tc * 8 + 5])
                for tc in range(OWN):
                    v.tensor_mul(TK[:, tc * 8 + 6:tc * 8 + 7], TK[:, tc * 8 + 2:tc * 8 + 3],
                                 TK[:, tc * 8 + 5:tc * 8 + 6])
                for tc in range(OWN):
                    v.tensor_mul(TK[:, tc * 8 + 7:tc * 8 + 8], TK[:, tc * 8 + 3:tc * 8 + 4],
                                 TK[:, tc * 8 + 5:tc * 8 + 6])
                for tc in range(OWN):
                    v.tensor_scalar(out=self.TKC1[:, tc * 8:(tc + 1) * 8],
                                    in0=self.TKM[:, tc * 8:(tc + 1) * 8],
                                    scalar1=TK[:, tc * 8 + 2:tc * 8 + 3],
                                    scalar2=TK[:, tc * 8 + 6:tc * 8 + 7],
                                    op0=ALU.is_equal, op1=ALU.mult)
                for tc in range(OWN):
                    v.tensor_scalar(out=self.TKE[:, tc * 8:(tc + 1) * 8],
                                    in0=self.TKM2[:, tc * 8:(tc + 1) * 8],
                                    scalar1=TK[:, tc * 8 + 3:tc * 8 + 4],
                                    scalar2=TK[:, tc * 8 + 7:tc * 8 + 8],
                                    op0=ALU.is_equal, op1=ALU.mult)
                for tc in range(OWN):
                    ad = v.tensor_add(self.COMB[:, tc * 8:(tc + 1) * 8],
                                      self.TKC1[:, tc * 8:(tc + 1) * 8],
                                      self.TKE[:, tc * 8:(tc + 1) * 8])
                    if tc == OWN - 1:
                        ad.then_inc(self.sV, 1)
            blk.vector(_v11b)
            vC = v_sco + 1
            v_comb = vC
            self.dump(blk, "d_comb", self.COMB[:], self.sV, v_comb)

            t12, v12, s12, qg12, qs12 = tC, vC, sC, qg, qs
            st12 = self.stC
            n_k, n_tc = 16, 8
            bank = lambda par, j: (self.pgen[j] if par == 0 else
                                   (self.pgen[4 + j] if j < 2 else self.pbank[j - 2]))

            t_marks = {}
            tt = t12
            for tb in range(NTB):
                for e in range(E):
                    for k in range(n_k):
                        tt += 1; t_marks[("gu", tb, e, k)] = tt
                    for sweep in range(4):
                        for ic in range(16):
                            tt += 1; t_marks[("w2", tb, e, sweep, ic)] = tt
            g_marks = {}
            gg = qg12
            for tb in range(NTB):
                for e in range(E):
                    for k in range(n_k):
                        gg += 16; g_marks[("w13g", tb, e, k)] = gg
                        gg += 16; g_marks[("w13u", tb, e, k)] = gg
                    for ic in range(16):
                        gg += 16; g_marks[("w2", tb, e, ic)] = gg
            vseq = []
            for tb in range(NTB):
                for e in range(E):
                    for k in range(n_k):
                        vseq.append(("act", tb, e, k))
                        if e == 0 and tb >= 1:
                            if k < n_tc:
                                vseq.append(("zst", tb - 1, k))
                            elif k == n_tc:
                                vseq.append(("var3", tb - 1, None))
                            else:
                                vseq.append(("norm", tb - 1, k - n_tc - 1))
                    if e == 0 and tb >= 1:
                        vseq.append(("norm", tb - 1, n_tc - 1))
                    for sweep in range(4):
                        vseq.append(("o2", tb, e, sweep))
                        if tb == NTB - 1 and e == E - 1:
                            vseq.append(("zst", tb, sweep * 2))
                            vseq.append(("zst", tb, sweep * 2 + 1))
            tbl = NTB - 1
            vseq.append(("var3", tbl, None))
            for tc in range(n_tc):
                vseq.append(("norm", tbl, tc))
            v_marks = {}
            vv2 = v12
            for op in vseq:
                vv2 += 1
                if op[0] == "act":
                    v_marks[("act", op[1], op[2], op[3])] = vv2
                elif op[0] == "o2":
                    v_marks[("o2", op[1], op[2], op[3])] = vv2
                elif op[0] == "var3":
                    v_marks[("var3", op[1])] = vv2
                elif op[0] == "norm":
                    v_marks[("norm", op[1], op[2])] = vv2
            s_marks = {}
            ss2 = s12
            for tb in range(NTB):
                for e in range(E):
                    for k in range(n_k):
                        ss2 += 1; s_marks[("silu", tb, e, k)] = ss2
                        if tb >= 1 and e == 0 and k == n_tc:
                            ss2 += 1; s_marks[("rs", tb - 1, 0)] = ss2
            ss2 += 1
            s_marks[("rs", tbl, 0)] = ss2
            st_marks = {}
            st_ = st12
            for tb in range(NTB):
                for tc in range(n_tc):
                    st_ += 16; st_marks[(tb, tc)] = st_

            def _g12(g):
                g.wait_ge(self.sV, v_y1)
                g.wait_ge(self.sT, t10 + 2 + OWN)
                w13cnt = 0
                for tb in range(NTB):
                    for e in range(E):
                        for k in range(n_k):
                            for half in range(2):
                                idx = w13cnt % 4
                                if w13cnt >= 4:
                                    kk = w13cnt - 4
                                    tbp, rem = divmod(kk, E * 2 * n_k)
                                    ep, rem2 = divmod(rem, 2 * n_k)
                                    kp = rem2 // 2
                                    g.wait_ge(self.sT, t_marks[("gu", tbp, ep, kp)])
                                m = k if half == 0 else 16 + k
                                g.dma_start(self.W13B[idx][:], self.w13b[e, m, :, :]
                                            ).then_inc(self.sQG, 16)
                                w13cnt += 1
                        if (tb, e) != (0, 0):
                            pe = tb * E + e - 1
                            ptb, pex = divmod(pe, E)
                            g.wait_ge(self.sT, t_marks[("w2", ptb, pex, 3, 15)])
                        for ic in range(16):
                            g.dma_start(self.W2ALL[:, ic * 1024:(ic + 1) * 1024],
                                        self.w2t[e, ic * 128:(ic + 1) * 128, :]
                                        ).then_inc(self.sQG, 16)
            blk.gpsimd(_g12)
            qg = gg

            def _t12(t):
                t.wait_ge(self.sV, vcast + 132 + OWN)
                for tb in range(NTB):
                    for e in range(E):
                        for k in range(n_k):
                            par = k % 2
                            if k >= 2:
                                t.wait_ge(self.sV, v_marks[("act", tb, e, k - 2)])
                            elif (tb, e) != (0, 0):
                                pe = tb * E + e - 1
                                ptb, pex = divmod(pe, E)
                                t.wait_ge(self.sV, v_marks[("o2", ptb, pex, 3)])
                            for half in range(2):
                                gm = g_marks[("w13g" if half == 0 else "w13u", tb, e, k)]
                                t.wait_ge(self.sQG, gm)
                                blkidx = ((tb * E + e) * n_k + k) * 2 + half
                                for ts in range(2):
                                    p = bank(par, 2 * half + ts)
                                    for dcx in range(8):
                                        lhs = self.W13B[blkidx % 4][:, dcx * 128:(dcx + 1) * 128]
                                        rhs = self.Y1FB[:, dcx * TOK + tb * TB + ts * 512:
                                                        dcx * TOK + tb * TB + ts * 512 + 512]
                                        mm = t.matmul(p[:], lhs, rhs,
                                                      start=(dcx == 0), stop=(dcx == 7))
                                        if half == 1 and ts == 1 and dcx == 7:
                                            mm.then_inc(self.sT, 1)
                        for sweep in range(4):
                            par = sweep % 2
                            if sweep == 0:
                                t.wait_ge(self.sV, v_marks[("act", tb, e, n_k - 2)])
                            elif sweep == 1:
                                t.wait_ge(self.sV, v_marks[("act", tb, e, n_k - 1)])
                            else:
                                t.wait_ge(self.sV, v_marks[("o2", tb, e, sweep - 2)])
                            for ic in range(16):
                                t.wait_ge(self.sQG, g_marks[("w2", tb, e, ic)])
                                for tci in range(2):
                                    tc = sweep * 2 + tci
                                    for ns in range(2):
                                        p = bank(par, 2 * tci + ns)
                                        lhs = self.ACT[:, ic * TB + tc * 128:
                                                       ic * TB + tc * 128 + 128]
                                        rhs = self.W2ALL[:, ic * 1024 + ns * 512:
                                                         ic * 1024 + (ns + 1) * 512 if ns else
                                                         ic * 1024 + 512]
                                        rhs = self.W2ALL[:, ic * 1024 + ns * 512:
                                                         ic * 1024 + ns * 512 + 512]
                                        mm = t.matmul(p[:], lhs, rhs,
                                                      start=(ic == 0), stop=(ic == 15))
                                        if tci == 1 and ns == 1:
                                            mm.then_inc(self.sT, 1)
            blk.tensor(_t12)
            tC = tt

            def _v12(v):
                if DEBUG:
                    v.wait_ge(self.sST, self.stC)
                fin = self.OUT2[E % 2]
                zb = self.OUT2[1 - (E % 2)]
                for op in vseq:
                    if op[0] == "act":
                        _, tb, e, k = op
                        par = k % 2
                        v.wait_ge(self.sS, s_marks[("silu", tb, e, k)])
                        for ts in range(2):
                            pup = bank(par, 2 + ts)
                            mu = v.tensor_mul(
                                self.ACT[:, k * TB + ts * 512: k * TB + ts * 512 + 512],
                                self.SG[:, par * 1024 + ts * 512: par * 1024 + (ts + 1) * 512],
                                pup[:])
                            if ts == 1:
                                mu.then_inc(self.sV, 1)
                    elif op[0] == "o2":
                        _, tb, e, sweep = op
                        par = sweep % 2
                        v.wait_ge(self.sT, t_marks[("w2", tb, e, sweep, 15)])
                        for tci in range(2):
                            tc = sweep * 2 + tci
                            ccol = self.COMB[:, (tb * 8 + tc) * 8 + e:
                                             (tb * 8 + tc) * 8 + e + 1]
                            for ns in range(2):
                                p = bank(par, 2 * tci + ns)
                                dst = self.OUT2[(e + 1) % 2][:, tc * D + ns * 512:
                                                             tc * D + ns * 512 + 512]
                                if e == 0:
                                    o = v.tensor_scalar(out=dst, in0=p[:], scalar1=ccol,
                                                        scalar2=None, op0=ALU.mult)
                                else:
                                    o = v.scalar_tensor_tensor(
                                        out=dst, in0=p[:], scalar=ccol,
                                        in1=self.OUT2[e % 2][:, tc * D + ns * 512:
                                                             tc * D + ns * 512 + 512],
                                        op0=ALU.mult, op1=ALU.add)
                                if tci == 1 and ns == 1:
                                    o.then_inc(self.sV, 1)
                    elif op[0] == "zst":
                        _, tbn, tc = op
                        v.tensor_add(zb[:, tc * D:(tc + 1) * D],
                                     self.Y1TB[:, (tbn * 8 + tc) * D:(tbn * 8 + tc + 1) * D],
                                     fin[:, tc * D:(tc + 1) * D])
                        v.reduce_sum(ST[:, tc:tc + 1], zb[:, tc * D:(tc + 1) * D], axis=AX.X)
                        v.scalar_tensor_tensor(
                            out=self.ln3scr[:], in0=zb[:, tc * D:(tc + 1) * D], scalar=0.0,
                            in1=zb[:, tc * D:(tc + 1) * D], op0=ALU.add, op1=ALU.mult,
                            accum_out=ST[:, 8 + tc:9 + tc]).then_inc(self.sV, 1)
                    elif op[0] == "var3":
                        self.vsync(v, v.tensor_scalar_mul(ST[:, 16:24], ST[:, 0:8], -1.0 / D))
                        v.tensor_mul(ST[:, 32:40], ST[:, 16:24], ST[:, 16:24])
                        self.vsync(v, v.tensor_scalar_mul(ST[:, 40:48], ST[:, 8:16], 1.0 / D))
                        v.tensor_sub(ST[:, 8:16], ST[:, 40:48], ST[:, 32:40]
                                     ).then_inc(self.sV, 1)
                    else:
                        _, tbn, tc = op
                        if tc == 0:
                            v.wait_ge(self.sS, s_marks[("rs", tbn, 0)])
                        if tbn > 0 or tc >= 2:
                            ptb, ptc = (tbn, tc - 2) if tc >= 2 else (tbn - 1, n_tc + tc - 2)
                            v.wait_ge(self.sST, st_marks[(ptb, ptc)])
                        v.tensor_scalar(
                            out=self.OUTB[tc % 2][:], in0=zb[:, tc * D:(tc + 1) * D],
                            scalar1=ST[:, 16 + tc:17 + tc], scalar2=ST[:, 24 + tc:25 + tc],
                            op0=ALU.add, op1=ALU.mult,
                        ).then_inc(self.sV, 1)
            blk.vector(_v12)
            vC = vv2

            def _s12(s):
                def rs_pair(tbn):
                    s.wait_ge(self.sV, v_marks[("var3", tbn)])
                    self.ssync(s, s.activation(ST[:, 40:48], ST[:, 8:16],
                                               AF.Ln, bias=self.epst[:], scale=1.0))
                    s.activation(ST[:, 24:32], ST[:, 40:48], AF.Exp,
                                 scale=-0.5).then_inc(self.sS, 1)
                for tb in range(NTB):
                    for e in range(E):
                        for k in range(n_k):
                            par = k % 2
                            s.wait_ge(self.sT, t_marks[("gu", tb, e, k)])
                            for ts in range(2):
                                pg = bank(par, 0 + ts)
                                ac = s.activation(
                                    self.SG[:, par * 1024 + ts * 512: par * 1024 + (ts + 1) * 512],
                                    pg[:], AF.Silu)
                                if ts == 1:
                                    ac.then_inc(self.sS, 1)
                            if tb >= 1 and e == 0 and k == n_tc:
                                rs_pair(tb - 1)
                rs_pair(tbl)
            blk.scalar(_s12)
            sC = ss2

            def _sy12(sy):
                for tb in range(NTB):
                    for tc in range(n_tc):
                        sy.wait_ge(self.sV, v_marks[("norm", tb, tc)])
                        row = (tb * 8 + tc) * 128
                        sy.dma_start(self.out[row:row + 128, :], self.OUTB[tc % 2][:]
                                     ).then_inc(self.sST, 16)
                sy.wait_ge(self.sST, st12 + NTB * n_tc * 16)
            blk.sync(_sy12)
            self.stC = st12 + NTB * n_tc * 16
        return self.nc


def _stage_inputs(L0_attn_x, L0_attn_norm_w, L0_attn_norm_b, L0_attn_qkv_w,
                  L0_attn_qkv_b, L0_attn_proj_w, L0_attn_proj_b, L0_ffn_gate_w,
                  L0_ffn_w13_weight, L0_ffn_w2_weight):
    x = np.asarray(L0_attn_x, dtype=np.float32)
    qkvT = np.ascontiguousarray(np.asarray(L0_attn_qkv_w, np.float32).T)
    qkvb = np.ascontiguousarray(np.asarray(L0_attn_qkv_b, np.float32).reshape(1, 3 * D))
    nw = np.ascontiguousarray(np.asarray(L0_attn_norm_w, np.float32).reshape(8, 128).T)
    nb = np.ascontiguousarray(np.asarray(L0_attn_norm_b, np.float32).reshape(8, 128).T)
    projT = np.ascontiguousarray(np.asarray(L0_attn_proj_w, np.float32).T)
    projb = np.ascontiguousarray(np.asarray(L0_attn_proj_b, np.float32).reshape(1, D))
    gateT = np.ascontiguousarray(np.asarray(L0_ffn_gate_w, np.float32).T)
    w13 = np.asarray(L0_ffn_w13_weight, np.float32)
    w13b = np.ascontiguousarray(
        w13.transpose(0, 2, 1).reshape(E, 8, 128, 32, 128).transpose(0, 3, 2, 1, 4)
        .reshape(E, 32, 128, 1024))
    w2t = np.ascontiguousarray(np.asarray(L0_ffn_w2_weight, np.float32).transpose(0, 2, 1))
    eye = np.eye(128, dtype=np.float32)
    shared = dict(qkvT=qkvT, qkvb=qkvb, nw=nw, nb=nb, projT=projT, projb=projb,
                  gateT=gateT, w13b=w13b, w2t=w2t, eye=eye)
    in_maps = []
    for c in range(NCORES):
        n, q = divmod(c, 4)
        xr = np.ascontiguousarray(np.roll(x[n], -q * TOK, axis=0))
        m = dict(shared)
        m["xfull"] = xr
        in_maps.append(m)
    return in_maps


def kernel(**inputs):
    in_maps = _stage_inputs(**inputs)
    mk = MK()
    nc = mk.build()
    res = run_bass_kernel_spmd(nc, in_maps, core_ids=list(range(NCORES)))
    out = np.empty((B, L, D), dtype=np.float32)
    for c in range(NCORES):
        n, q = divmod(c, 4)
        out[n, q * TOK:(q + 1) * TOK, :] = res.results[c]["out"]
    return out


# version 18 (final)
# speedup vs baseline: 1.1101x; 1.0063x over previous
import numpy as np
import concourse.bass as bass
import concourse.mybir as mybir
from concourse.bass_utils import run_bass_kernel_spmd

F32 = mybir.dt.float32
BF16 = mybir.dt.bfloat16
AF = mybir.ActivationFunctionType
ALU = mybir.AluOpType
AX = mybir.AxisListType

DEBUG = False

B, L, D = 2, 8192, 1024
NCORES = 8
TOK = 2048
CH = 64
OWN = 16
DC = 8
DEW = 1032
DECN = 9
E = 8
INTER = 2048
TB = 1024
NTB = 2
SCALE = float(np.float32(1.0 / np.sqrt(L)))
EPS = 1e-5
PADCYC = 384


class MK:
    def __init__(self):
        nc = bass.Bass("TRN2", target_bir_lowering=False, detect_race_conditions=False)
        self.nc = nc
        d = nc.dram_tensor
        self.xfull = d("xfull", [L, D], F32, kind="ExternalInput").ap()
        self.qkvT = d("qkvT", [D, 3 * D], F32, kind="ExternalInput").ap()
        self.qkvb = d("qkvb", [1, 3 * D], F32, kind="ExternalInput").ap()
        self.nw = d("nw", [128, 8], F32, kind="ExternalInput").ap()
        self.nb = d("nb", [128, 8], F32, kind="ExternalInput").ap()
        self.projT = d("projT", [D, D], F32, kind="ExternalInput").ap()
        self.projb = d("projb", [1, D], F32, kind="ExternalInput").ap()
        self.gateT = d("gateT", [D, E], F32, kind="ExternalInput").ap()
        self.w13b = d("w13b", [E, 32, 128, 1024], F32, kind="ExternalInput").ap()
        self.w2t = d("w2t", [E, INTER, D], F32, kind="ExternalInput").ap()
        self.eye = d("eye", [128, 128], F32, kind="ExternalInput").ap()
        self.out = d("out", [TOK, D], F32, kind="ExternalOutput").ap()
        self.dbg = {}
        if DEBUG:
            for name, shape, dt in [
                ("d_ht", [128, CH * DEW], BF16), ("d_gs", [128, DECN * DEW], BF16),
                ("d_m1t", [128, DECN * D], BF16), ("d_wmat", [128, 8 * D], BF16),
                ("d_rsw", [128, 8], F32), ("d_vf", [128, 8 * TOK], BF16),
                ("d_at", [128, 8 * TOK], BF16), ("d_y1", [128, OWN * D], F32),
                ("d_lgt", [8, TOK], F32), ("d_comb", [128, OWN * E], F32),
                ("d_wqx", [128, DECN * D], BF16), ("d_wvx", [128, DECN * D], BF16),
                ("d_hof", [128, DECN * TOK], BF16),
            ]:
                self.dbg[name] = d(name, shape, dt, kind="ExternalOutput").ap()

    def alloc(self, ctx):
        nc = self.nc
        ent = ctx.enter_context
        self.eyeb = ent(nc.sbuf_tensor("eyeb", [128, 128], BF16))
        self.eyef = ent(nc.sbuf_tensor("eyef", [128, 128], F32))
        self.nwc = ent(nc.sbuf_tensor("nwc", [128, 8], F32))
        self.nbc = ent(nc.sbuf_tensor("nbc", [128, 8], F32))
        self.epst = ent(nc.sbuf_tensor("epst", [128, 1], F32))
        self.stats = ent(nc.sbuf_tensor("stats", [128, 256], F32))
        self.rsw = ent(nc.sbuf_tensor("rsw", [128, 8], F32))
        self.gatets = ent(nc.sbuf_tensor("gatets", [128, 64], F32))
        self.onesf = ent(nc.sbuf_tensor("onesf", [1, 128], F32))

        SLAB = 193536
        self.slab = ent(nc.sbuf_tensor("slab", [128, SLAB], mybir.dt.uint8))
        base = nc.lookup_mloc(self.slab).addr
        self._ranges = []

        def ov(name, off, shape, dtype):
            per = int(np.prod(shape[1:])) * mybir.dt.size(dtype)
            assert off % 32 == 0, name
            assert off + per <= SLAB, (name, off, per)
            self._ranges.append((name, off, off + per))
            hnd = nc.virtual_sbuf_tensor(name, shape, dtype)
            nc._tensor(hnd.name, hnd.shape, hnd.dtype, type="SB")
            mloc = nc.lookup_mloc(hnd)
            mloc.allocated = True
            mloc.addr = base + off
            return hnd

        self.HT = ov("HT", 0, [128, CH * DEW], BF16)
        self.GS = ov("GS", 132096, [128, DECN * DEW], BF16)
        self.HOF = ov("HOF", 156672, [128, DECN * TOK], BF16)
        self.xbuf = [ov(f"xbuf{i}", 132096 + i * 4096, [128, D], F32) for i in range(6)]
        self.WQX = ov("WQX", 0, [128, DECN * D], BF16)
        self.WKX = ov("WKX", 18432, [128, DECN * D], BF16)
        self.WVX = ov("WVX", 36864, [128, DECN * D], BF16)
        self.M1T = ov("M1T", 55296, [128, DECN * D], BF16)
        self.WMAT = ov("WMAT", 73728, [128, 8 * D], BF16)
        self.WT = ov("WT", 90112, [128, 8 * D], BF16)
        self.qbuf = [ov(f"qbuf{i}", 106496 + i * 12288, [128, 3 * D], F32) for i in range(2)]
        self.brows = ent(nc.sbuf_tensor("brows", [1, 4 * D], F32))
        self.qkvbS = self.brows[0:1, 0:3 * D]
        self.projbR = self.brows[0:1, 3 * D:4 * D]
        self.VF = ov("VF", 0, [128, 8 * TOK], BF16)
        self.AT = ov("AT", 132096, [128, 8 * TOK], BF16)
        self.XOWN = ov("XOWN", 0, [128, OWN * D], F32)
        self.Y1TOKF = ov("Y1TOKF", 65536, [128, OWN * D], F32)
        self.PJ = ov("PJ", 164864, [128, 8 * D], BF16)
        self.projbB = ov("projbB", 181248, [128, D], F32)
        self.z1pre = ov("z1pre", 189440, [128, D], F32)
        self.z1b = ov("z1b", 185344, [128, D], F32)
        self.Y1TB = ov("Y1TB", 0, [128, OWN * D], BF16)
        self.Y1FB = ov("Y1FB", 32768, [128, 8 * TOK], BF16)
        self.LGT = ov("LGT", 131072, [8, TOK], F32)
        self.strip = [ov(f"strip{i}", 139264 + i * 2048, [128, 512], F32) for i in range(2)]
        self.SCO = ov("SCO", 143360, [128, OWN * E], F32)
        self.EXPS = ov("EXPS", 143872, [128, OWN * E], F32)
        self.TKG = ov("TKG", 144384, [128, OWN * 4], F32)
        self.TKG2 = ov("TKG2", 144640, [128, OWN * 4], F32)
        self.TKM = ov("TKM", 144896, [128, OWN * E], F32)
        self.TKM2 = ov("TKM2", 145408, [128, OWN * E], F32)
        self.TKE = ov("TKE", 145920, [128, OWN * E], F32)
        self.TKC1 = ov("TKC1", 146432, [128, OWN * E], F32)
        self.TKS = ov("TKS", 146944, [128, OWN * 8], F32)
        self.COMB = ov("COMB", 147456, [128, OWN * E], F32)
        self.ACT = ov("ACT", 65536, [128, 16 * TB], BF16)
        self.OUT2 = [ov("OUT2a", 98304, [128, 8 * D], BF16),
                     ov("OUT2b", 114688, [128, 8 * D], BF16)]
        self.W2ALL = ov("W2ALL", 148480, [128, 16 * 1024], BF16)
        self.W13B = [ov(f"W13B{i}", 181248 + i * 2048, [128, 1024], BF16) for i in range(4)]
        self.SG = ov("SG", 189440, [128, 2048], BF16)
        self.Z2 = [ov("Z2x", 143360, [128, D], F32)]
        self.ln3scr = ov("ln3scr", 139264, [128, D], F32)
        self.OUTB = [ov(f"OUTB{i}", 131072 + i * 4096, [128, D], F32) for i in range(2)]

        self.pgen = [ent(nc.psum_tensor(f"pg{i}", [128, 512], F32)) for i in range(6)]
        self.pbank = [ent(nc.psum_tensor(f"pb{i}", [128, 512], F32)) for i in range(2)]

        def pov(name, i, shape, dtype):
            h = nc.virtual_psum_tensor(name, shape, dtype)
            nc._tensor(h.name, h.shape, h.dtype, type="PSUM")
            m = nc.lookup_mloc(h)
            m.allocated = True
            m.bank = nc.lookup_mloc(self.pbank[i]).bank
            m.addr = 0
            return h

        self.pt = [pov(f"ptx{i}", i, [128, 128], BF16) for i in range(2)]
        self.ptf = [pov(f"ptfx{i}", i, [128, 128], F32) for i in range(2)]

        def pov2(name, src_t, shape, dtype):
            h = nc.virtual_psum_tensor(name, shape, dtype)
            nc._tensor(h.name, h.shape, h.dtype, type="PSUM")
            m = nc.lookup_mloc(h)
            m.allocated = True
            m.bank = nc.lookup_mloc(src_t).bank
            m.addr = 0
            return h

        self.pgc = [pov2(f"pgcx{i}", self.pgen[4 + i], [128, 16], F32) for i in range(2)]
        sem = lambda n: ent(nc.semaphore(n))
        self.sT, self.sV, self.sS = sem("sT"), sem("sV"), sem("sS")
        self.sQS, self.sQG, self.sST = sem("sQS"), sem("sQG"), sem("sST")
        self.sVx, self.sSx = sem("sVx"), sem("sSx")
        self.vx_cnt, self.sx_cnt = 0, 0

    def vsync(self, v, inst):
        self.vx_cnt += 1
        inst.then_inc(self.sVx, 1)
        v.wait_ge(self.sVx, self.vx_cnt)

    def ssync(self, s, inst):
        self.sx_cnt += 1
        inst.then_inc(self.sSx, 1)
        s.wait_ge(self.sSx, self.sx_cnt)

    def dump(self, blk, name, src_ap, wait_sem, wait_val):
        if not DEBUG or name not in self.dbg:
            return
        dst = self.dbg[name]
        st0 = self.stC

        def _sy(sy):
            sy.wait_ge(wait_sem, wait_val)
            sy.dma_start(dst, src_ap).then_inc(self.sST, 16)
            sy.wait_ge(self.sST, st0 + 16)
        blk.sync(_sy)
        self.stC = st0 + 16

    def build(self):
        from contextlib import ExitStack
        with ExitStack() as ctx:
            self.alloc(ctx)
            blk = ctx.enter_context(self.nc.Block())
            self.stC = 0

            qs = 0

            def _sy0(sy):
                sy.dma_start(self.eyef[:], self.eye).then_inc(self.sQS, 16)
                sy.dma_start(self.nwc[:], self.nw).then_inc(self.sQS, 16)
                sy.dma_start(self.nbc[:], self.nb).then_inc(self.sQS, 16)
                src = self.gateT.rearrange("(c p) e -> p c e", p=128)
                dst = self.gatets.rearrange("p (c e) -> p c e", c=8)
                sy.dma_start(dst, src).then_inc(self.sQS, 16)
                sy.dma_start(self.qkvbS, self.qkvb).then_inc(self.sQS, 16)
                sy.dma_start(self.projbR, self.projb).then_inc(self.sQS, 16)
            blk.sync(_sy0)
            qs += 6 * 16
            qg = 0

            def _g0(g):
                g.dma_start(self.eyeb[:], self.eye).then_inc(self.sQG, 16)
            blk.gpsimd(_g0)
            qg += 16

            vC = 0

            def _v0(v):
                v.memset(self.epst[:], EPS)
                v.memset(self.onesf[:], 1.0)
                v.memset(self.HT[:, 0:CH * DEW // 2], 0.0)
                v.memset(self.HT[:, CH * DEW // 2:], 0.0)
                ones_cols = self.HT.rearrange("p (c w) -> p c w", c=CH)[:, :, 1024:1025]
                v.memset(ones_cols, 1.0).then_inc(self.sV, 1)
            blk.vector(_v0)
            vC += 1

            qs1 = qs
            v1 = vC
            ST = self.stats
            lngr = [(g * 3, min(g * 3 + 3, CH)) for g in range((CH + 2) // 3)]
            NG = len(lngr)
            vA, vB, v_norm = {}, {}, {}
            vv = v1
            for gi, (a, b) in enumerate(lngr):
                if gi >= 2:
                    for cc in range(*lngr[gi - 2]):
                        vv += 1; v_norm[cc] = vv
                for cc in range(a, b):
                    vv += 1; vA[cc] = vv
                if gi >= 1:
                    vv += 1; vB[gi - 1] = vv
            vv += 1; vB[NG - 1] = vv
            for gg in (NG - 2, NG - 1):
                for cc in range(*lngr[gg]):
                    vv += 1; v_norm[cc] = vv
            sA, sR = {}, {}
            ss = 0
            for gi, (a, b) in enumerate(lngr):
                for cc in range(a, b):
                    ss += 1; sA[cc] = ss
                if gi >= 1:
                    ss += 1; sR[gi - 1] = ss
            ss += 1; sR[NG - 1] = ss

            def _sy1(sy):
                for cc in range(CH):
                    if cc >= 6:
                        sy.wait_ge(self.sV, v_norm[cc - 6])
                    sy.dma_start(self.xbuf[cc % 6][:],
                                 self.xfull[cc * 128:(cc + 1) * 128, :]).then_inc(self.sQS, 16)
            blk.sync(_sy1)
            qs = qs1 + CH * 16

            def _norm1(v, gg):
                v.wait_ge(self.sS, sR[gg])
                for cc in range(*lngr[gg]):
                    v.tensor_scalar(
                        out=self.HT[:, cc * DEW:cc * DEW + D], in0=self.xbuf[cc % 6][:],
                        scalar1=ST[:, 128 + cc:129 + cc], scalar2=ST[:, 192 + cc:193 + cc],
                        op0=ALU.add, op1=ALU.mult,
                    ).then_inc(self.sV, 1)

            def _smalls1(v, gg):
                a, b = lngr[gg]
                v.wait_ge(self.sS, sA[b - 1])
                self.vsync(v, v.tensor_scalar_mul(ST[:, 128 + a:128 + b],
                                                  ST[:, a:b], -1.0 / D))
                v.tensor_mul(ST[:, a:b], ST[:, 128 + a:128 + b], ST[:, 128 + a:128 + b])
                self.vsync(v, v.tensor_scalar_mul(ST[:, 192 + a:192 + b],
                                                  ST[:, 64 + a:64 + b], 1.0 / D))
                v.tensor_sub(ST[:, 64 + a:64 + b], ST[:, 192 + a:192 + b],
                             ST[:, a:b]).then_inc(self.sV, 1)

            def _v1(v):
                for gi, (a, b) in enumerate(lngr):
                    if gi >= 2:
                        _norm1(v, gi - 2)
                    for cc in range(a, b):
                        v.wait_ge(self.sQS, qs1 + (cc + 1) * 16)
                        v.reduce_sum(ST[:, cc:cc + 1], self.xbuf[cc % 6][:], axis=AX.X
                                     ).then_inc(self.sV, 1)
                    if gi >= 1:
                        _smalls1(v, gi - 1)
                _smalls1(v, NG - 1)
                _norm1(v, NG - 2)
                _norm1(v, NG - 1)
            blk.vector(_v1)
            vC = vv

            def _s1(s):
                for gi, (a, b) in enumerate(lngr):
                    for cc in range(a, b):
                        s.wait_ge(self.sQS, qs1 + (cc + 1) * 16)
                        s.activation(self.HT[:, cc * DEW:cc * DEW + D], self.xbuf[cc % 6][:],
                                     AF.Square, accum_out=ST[:, 64 + cc:65 + cc]
                                     ).then_inc(self.sS, 1)
                    if gi >= 1:
                        s.wait_ge(self.sV, vB[gi - 1])
                        self.ssync(s, s.activation(ST[:, lngr[gi - 1][0]:lngr[gi - 1][1]],
                                                   ST[:, 64 + lngr[gi - 1][0]:64 + lngr[gi - 1][1]],
                                                   AF.Ln, bias=self.epst[:], scale=1.0))
                        s.activation(ST[:, 192 + lngr[gi - 1][0]:192 + lngr[gi - 1][1]],
                                     ST[:, lngr[gi - 1][0]:lngr[gi - 1][1]], AF.Exp, scale=-0.5
                                     ).then_inc(self.sS, 1)
                s.wait_ge(self.sV, vB[NG - 1])
                self.ssync(s, s.activation(ST[:, lngr[NG - 1][0]:lngr[NG - 1][1]],
                                           ST[:, 64 + lngr[NG - 1][0]:64 + lngr[NG - 1][1]],
                                           AF.Ln, bias=self.epst[:], scale=1.0))
                s.activation(ST[:, 192 + lngr[NG - 1][0]:192 + lngr[NG - 1][1]],
                             ST[:, lngr[NG - 1][0]:lngr[NG - 1][1]], AF.Exp, scale=-0.5
                             ).then_inc(self.sS, 1)
            blk.scalar(_s1)
            sC = ss
            tC = 0
            t3, v3 = tC, vC

            def _t3(t):
                for i in range(OWN * 8):
                    cc, dcx = divmod(i, 8)
                    t.wait_ge(self.sV, v_norm[cc])
                    if i >= 2:
                        t.wait_ge(self.sV, v3 + i - 1)
                    t.transpose(self.pt[i % 2][:],
                                self.HT[:, cc * DEW + dcx * 128: cc * DEW + dcx * 128 + 128],
                                self.eyeb[:]).then_inc(self.sT, 1)
            blk.tensor(_t3)
            tC = t3 + OWN * 8

            def _v3(v):
                for i in range(OWN * 8):
                    cc, dcx = divmod(i, 8)
                    v.wait_ge(self.sT, t3 + i + 1)
                    v.tensor_copy(
                        self.HOF[:, dcx * TOK + cc * 128: dcx * TOK + cc * 128 + 128],
                        self.pt[i % 2][:]).then_inc(self.sV, 1)
                v.memset(self.HOF[0:8, 8 * TOK: 9 * TOK], 0.0)
                v.memset(self.HOF[0:1, 8 * TOK: 9 * TOK], 1.0).then_inc(self.sV, 1)
            blk.vector(_v3)
            vC = v3 + OWN * 8 + 1
            v_hof = vC
            self.dump(blk, "d_hof", self.HOF[:], self.sV, v_hof)

            t2, v2 = tC, vC

            def _t2(t):
                t.wait_ge(self.sV, v_norm[CH - 1])
                for mc in range(DECN):
                    M = 128 if mc < 8 else 8
                    pa = self.pgen[2 * (mc % 2)]
                    pb = self.pgen[2 * (mc % 2) + 1]
                    pc_ = self.pgc[mc % 2]
                    if mc >= 2:
                        t.wait_ge(self.sV, v2 + mc - 1)
                    for cc in range(CH):
                        lhs = self.HT[:, cc * DEW + mc * 128: cc * DEW + mc * 128 + M]
                        st_, sp = (cc == 0), (cc == CH - 1)
                        t.matmul(pa[0:M, :], lhs, self.HT[:, cc * DEW: cc * DEW + 512],
                                 start=st_, stop=sp)
                        t.matmul(pb[0:M, :], lhs, self.HT[:, cc * DEW + 512: cc * DEW + 1024],
                                 start=st_, stop=sp)
                        mm = t.matmul(pc_[0:M, 0:8], lhs,
                                      self.HT[:, cc * DEW + 1024: cc * DEW + 1032],
                                      start=st_, stop=sp)
                        if sp:
                            mm.then_inc(self.sT, 1)
            blk.tensor(_t2)
            tC = t2 + DECN

            def _v2(v):
                for mc in range(DECN):
                    M = 128 if mc < 8 else 8
                    pa = self.pgen[2 * (mc % 2)]
                    pb = self.pgen[2 * (mc % 2) + 1]
                    pc_ = self.pgc[mc % 2]
                    v.wait_ge(self.sT, t2 + mc + 1)
                    o = mc * DEW
                    v.tensor_copy(self.GS[0:M, o:o + 512], pa[0:M, :])
                    v.tensor_copy(self.GS[0:M, o + 512:o + 1024], pb[0:M, :])
                    v.tensor_copy(self.GS[0:M, o + 1024:o + 1032], pc_[0:M, 0:8]
                                  ).then_inc(self.sV, 1)
            blk.vector(_v2)
            vC = v2 + DECN
            v_g = vC
            self.dump(blk, "d_gs", self.GS[:], self.sV, v_g)

            qs4, v4, t4 = qs, vC, tC

            def _sy4(sy):
                sy.wait_ge(self.sT, t2 + DECN)
                for dcx in range(DC):
                    if dcx >= 2:
                        sy.wait_ge(self.sV, v4 + dcx - 1)
                        sy.wait_ge(self.sT, t4 + dcx - 1)
                    sy.dma_start(self.qbuf[dcx % 2][:],
                                 self.qkvT[dcx * 128:(dcx + 1) * 128, :]).then_inc(self.sQS, 16)
            blk.sync(_sy4)
            qs = qs4 + DC * 16

            def _v4(v):
                v.wait_ge(self.sT, t2 + DECN)
                if DEBUG:
                    v.wait_ge(self.sST, self.stC)
                for dcx in range(DC):
                    qb = self.qbuf[dcx % 2]
                    v.wait_ge(self.sQS, qs4 + (dcx + 1) * 16)
                    g = self.nwc[:, dcx:dcx + 1]
                    v.tensor_scalar(out=self.WQX[:, dcx * D:(dcx + 1) * D], in0=qb[:, 0:D],
                                    scalar1=g, scalar2=None, op0=ALU.mult)
                    v.tensor_scalar(out=self.WKX[:, dcx * D:(dcx + 1) * D], in0=qb[:, D:2 * D],
                                    scalar1=g, scalar2=None, op0=ALU.mult)
                    v.tensor_scalar(out=self.WVX[:, dcx * D:(dcx + 1) * D],
                                    in0=qb[:, 2 * D:3 * D],
                                    scalar1=g, scalar2=None, op0=ALU.mult).then_inc(self.sV, 1)
            blk.vector(_v4)
            vC = v4 + DC

            def _t4(t):
                for dcx in range(DC):
                    t.wait_ge(self.sQS, qs4 + (dcx + 1) * 16)
                    for j in range(6):
                        mm = t.matmul(self.pgen[j][0:1, :], self.nbc[:, dcx:dcx + 1],
                                      self.qbuf[dcx % 2][:, j * 512:(j + 1) * 512],
                                      start=(dcx == 0), stop=(dcx == DC - 1))
                        if j == 5:
                            mm.then_inc(self.sT, 1)
            blk.tensor(_t4)
            tC = t4 + DC

            def _v4b(v):
                v.wait_ge(self.sT, t4 + DC)
                for w in (self.WQX, self.WKX, self.WVX):
                    v.memset(w[0:8, 8 * D: 9 * D], 0.0)
                for j, (w, boff) in enumerate([(self.WQX, 0), (self.WQX, 512),
                                               (self.WKX, 1024), (self.WKX, 1536),
                                               (self.WVX, 2048), (self.WVX, 2560)]):
                    col = boff % 1024
                    v.tensor_add(w[0:1, 8 * D + col: 8 * D + col + 512],
                                 self.pgen[j][0:1, :], self.brows[0:1, boff:boff + 512]
                                 ).then_inc(self.sV, 1 if j == 5 else 1)
            blk.vector(_v4b)
            vC += 6
            v_wx = vC
            self.dump(blk, "d_wqx", self.WQX[:], self.sV, v_wx)
            self.dump(blk, "d_wvx", self.WVX[:], self.sV, v_wx)

            t5, v5 = tC, vC

            def _t5(t):
                t.wait_ge(self.sV, v_wx)
                for rc in range(DECN):
                    Mr = 128 if rc < 8 else 8
                    if rc >= 2:
                        t.wait_ge(self.sV, v5 + rc - 1)
                    for ns in range(2):
                        p = self.pgen[2 * (rc % 2) + ns]
                        for dcx in range(DECN):
                            K = 128 if dcx < 8 else 8
                            lhs = self.GS[0:K, dcx * DEW + rc * 128: dcx * DEW + rc * 128 + Mr]
                            rhs = self.WQX[0:K, dcx * D + ns * 512: dcx * D + ns * 512 + 512]
                            mm = t.matmul(p[0:Mr, :], lhs, rhs,
                                          start=(dcx == 0), stop=(dcx == DECN - 1))
                            if ns == 1 and dcx == DECN - 1:
                                mm.then_inc(self.sT, 1)
            blk.tensor(_t5)
            tC = t5 + DECN

            def _v5(v):
                for rc in range(DECN):
                    Mr = 128 if rc < 8 else 8
                    v.wait_ge(self.sT, t5 + rc + 1)
                    for ns in range(2):
                        p = self.pgen[2 * (rc % 2) + ns]
                        cp = v.tensor_copy(
                            self.M1T[0:Mr, rc * D + ns * 512: rc * D + ns * 512 + 512],
                            p[0:Mr, :])
                        if ns == 1:
                            cp.then_inc(self.sV, 1)
            blk.vector(_v5)
            vC = v5 + DECN
            v_m1 = vC
            self.dump(blk, "d_m1t", self.M1T[:], self.sV, v_m1)

            t6, v6, s6 = tC, vC, sC

            def _t6(t):
                t.wait_ge(self.sV, v_m1)
                for ic in range(8):
                    if ic >= 2:
                        t.wait_ge(self.sS, s6 + ic - 1)
                    for ns in range(2):
                        p = self.pgen[2 * (ic % 2) + ns]
                        for dcx in range(DECN):
                            K = 128 if dcx < 8 else 8
                            lhs = self.M1T[0:K, dcx * D + ic * 128: dcx * D + ic * 128 + 128]
                            rhs = self.WKX[0:K, dcx * D + ns * 512: dcx * D + ns * 512 + 512]
                            mm = t.matmul(p[:], lhs, rhs,
                                          start=(dcx == 0), stop=(dcx == DECN - 1))
                            if ns == 1 and dcx == DECN - 1:
                                mm.then_inc(self.sT, 1)
            blk.tensor(_t6)
            tC = t6 + 8

            def _v6(v):
                for ic in range(8):
                    pa = self.pgen[2 * (ic % 2)]
                    pb = self.pgen[2 * (ic % 2) + 1]
                    v.wait_ge(self.sT, t6 + ic + 1)
                    v.reduce_max(ST[:, 20:21], pa[:], axis=AX.X, negate=True)
                    self.vsync(v, v.reduce_max(ST[:, 21:22], pb[:], axis=AX.X, negate=True))
                    self.vsync(v, v.tensor_tensor(ST[:, 22:23], ST[:, 20:21],
                                                  ST[:, 21:22], ALU.min))
                    v.tensor_scalar_mul(ST[:, 24:25], ST[:, 22:23], SCALE).then_inc(self.sV, 1)
                    v.wait_ge(self.sS, s6 + ic + 1)
                    self.vsync(v, v.reduce_sum(ST[:, 23:24],
                                               self.WMAT[:, ic * D:(ic + 1) * D], axis=AX.X))
                    v.reciprocal(self.rsw[:, ic:ic + 1], ST[:, 23:24]).then_inc(self.sV, 1)
            blk.vector(_v6)
            vC = v6 + 16

            def _s6(s):
                for ic in range(8):
                    pa = self.pgen[2 * (ic % 2)]
                    pb = self.pgen[2 * (ic % 2) + 1]
                    s.wait_ge(self.sV, v6 + 2 * ic + 1)
                    s.activation(self.WMAT[:, ic * D: ic * D + 512], pa[:], AF.Exp,
                                 bias=ST[:, 24:25], scale=SCALE)
                    s.activation(self.WMAT[:, ic * D + 512: ic * D + 1024], pb[:], AF.Exp,
                                 bias=ST[:, 24:25], scale=SCALE).then_inc(self.sS, 1)
            blk.scalar(_s6)
            sC = s6 + 8
            v_soft = vC
            self.dump(blk, "d_wmat", self.WMAT[:], self.sS, sC)
            self.dump(blk, "d_rsw", self.rsw[:], self.sV, v_soft)

            t7, v7 = tC, vC

            def _t7(t):
                t.wait_ge(self.sV, v_soft)
                for i in range(64):
                    ic, jc = divmod(i, 8)
                    if i >= 2:
                        t.wait_ge(self.sV, v7 + i - 1)
                    t.transpose(self.pt[i % 2][:],
                                self.WMAT[:, ic * D + jc * 128: ic * D + jc * 128 + 128],
                                self.eyeb[:]).then_inc(self.sT, 1)
            blk.tensor(_t7)
            tC = t7 + 64

            def _v7(v):
                for i in range(64):
                    ic, jc = divmod(i, 8)
                    v.wait_ge(self.sT, t7 + i + 1)
                    v.tensor_copy(self.WT[:, jc * D + ic * 128: jc * D + ic * 128 + 128],
                                  self.pt[i % 2][:]).then_inc(self.sV, 1)
            blk.vector(_v7)
            vC = v7 + 64
            v_wt = vC

            t8, v8 = tC, vC

            def _t8(t):
                t.wait_ge(self.sV, v_wt)
                for i in range(32):
                    vc_, cs = divmod(i, 4)
                    if i >= 2:
                        t.wait_ge(self.sV, v8 + i - 1)
                    p = self.pgen[4 + (i % 2)]
                    for dcx in range(DECN):
                        K = 128 if dcx < 8 else 8
                        lhs = self.WVX[0:K, dcx * D + vc_ * 128: dcx * D + vc_ * 128 + 128]
                        rhs = self.HOF[0:K, dcx * TOK + cs * 512: dcx * TOK + cs * 512 + 512]
                        mm = t.matmul(p[:], lhs, rhs, start=(dcx == 0), stop=(dcx == DECN - 1))
                        if dcx == DECN - 1:
                            mm.then_inc(self.sT, 1)
            blk.tensor(_t8)
            tC = t8 + 32

            def _v8(v):
                if DEBUG:
                    v.wait_ge(self.sST, self.stC)
                for i in range(32):
                    vc_, cs = divmod(i, 4)
                    v.wait_ge(self.sT, t8 + i + 1)
                    p = self.pgen[4 + (i % 2)]
                    v.tensor_copy(self.VF[:, vc_ * TOK + cs * 512: vc_ * TOK + cs * 512 + 512],
                                  p[:]).then_inc(self.sV, 1)
            blk.vector(_v8)
            vC = v8 + 32
            v_vf = vC
            self.dump(blk, "d_vf", self.VF[:], self.sV, v_vf)

            t9, v9 = tC, vC

            def _t9(t):
                t.wait_ge(self.sV, v_vf)
                if DEBUG:
                    t.wait_ge(self.sST, self.stC)
                for i in range(32):
                    ic, cs = divmod(i, 4)
                    if i >= 4:
                        t.wait_ge(self.sV, v9 + i - 3)
                    p = self.pgen[i % 4]
                    for jc in range(8):
                        lhs = self.WT[:, jc * D + ic * 128: jc * D + ic * 128 + 128]
                        rhs = self.VF[:, jc * TOK + cs * 512: jc * TOK + cs * 512 + 512]
                        mm = t.matmul(p[:], lhs, rhs, start=(jc == 0), stop=(jc == 7))
                        if jc == 7:
                            mm.then_inc(self.sT, 1)
            blk.tensor(_t9)
            tC = t9 + 32

            def _v9(v):
                if DEBUG:
                    v.wait_ge(self.sST, self.stC)
                for i in range(32):
                    ic, cs = divmod(i, 4)
                    v.wait_ge(self.sT, t9 + i + 1)
                    p = self.pgen[i % 4]
                    v.tensor_scalar(
                        out=self.AT[:, ic * TOK + cs * 512: ic * TOK + cs * 512 + 512],
                        in0=p[:], scalar1=self.rsw[:, ic:ic + 1], scalar2=None,
                        op0=ALU.mult).then_inc(self.sV, 1)
            blk.vector(_v9)
            vC = v9 + 32
            v_at = vC
            self.dump(blk, "d_at", self.AT[:], self.sV, v_at)

            t10, v10, s10, qs10, qg10 = tC, vC, sC, qs, qg

            def _sy10(sy):
                sy.wait_ge(self.sV, v_at)
                if DEBUG:
                    sy.wait_ge(self.sST, self.stC)
                for cc in range(OWN):
                    sy.dma_start(self.XOWN[:, cc * D:(cc + 1) * D],
                                 self.xfull[cc * 128:(cc + 1) * 128, :]).then_inc(self.sQS, 16)
            blk.sync(_sy10)
            qs = qs10 + OWN * 16

            def _g10(g):
                g.wait_ge(self.sV, v_vf)
                if DEBUG:
                    g.wait_ge(self.sST, self.stC)
                for qc in range(8):
                    g.dma_start(self.PJ[:, qc * D:(qc + 1) * D],
                                self.projT[qc * 128:(qc + 1) * 128, :]).then_inc(self.sQG, 16)
            blk.gpsimd(_g10)
            qg = qg10 + 8 * 16

            v_st10 = {cc: v10 + 1 + 2 * cc + 1 for cc in range(OWN)}
            v_nm10 = {cc: v10 + 1 + 2 * cc + 2 for cc in range(OWN)}
            s_e10 = {cc: s10 + cc + 1 for cc in range(OWN)}

            def _t10(t):
                t.wait_ge(self.sV, v_at)
                t.wait_ge(self.sQS, qs10 + 16)
                for ns in range(2):
                    mm = t.matmul(self.pgen[4 + ns][:], self.onesf[:],
                                  self.brows[0:1, 3 * D + ns * 512: 3 * D + (ns + 1) * 512],
                                  start=True, stop=True)
                    mm.then_inc(self.sT, 1)
                t.wait_ge(self.sQG, qg10 + 8 * 16)
                for cc in range(OWN):
                    if cc >= 2:
                        t.wait_ge(self.sV, v_st10[cc - 2])
                    for ns in range(2):
                        p = self.pgen[2 * (cc % 2) + ns]
                        for qc in range(8):
                            lhs = self.AT[:, qc * TOK + cc * 128: qc * TOK + cc * 128 + 128]
                            rhs = self.PJ[:, qc * D + ns * 512: qc * D + ns * 512 + 512]
                            mm = t.matmul(p[:], lhs, rhs, start=(qc == 0), stop=(qc == 7))
                            if ns == 1 and qc == 7:
                                mm.then_inc(self.sT, 1)
            blk.tensor(_t10)
            tC = t10 + 2 + OWN

            def _v10(v):
                v.wait_ge(self.sT, t10 + 2)
                v.tensor_copy(self.projbB[:, 0:512], self.pgen[4][:])
                v.tensor_copy(self.projbB[:, 512:1024], self.pgen[5][:]).then_inc(self.sV, 1)
                for cc in range(OWN):
                    r = cc % 4
                    v.wait_ge(self.sT, t10 + 2 + cc + 1)
                    v.wait_ge(self.sQS, qs10 + (cc + 1) * 16)
                    xo = self.XOWN[:, cc * D:(cc + 1) * D]
                    for ns in range(2):
                        p = self.pgen[2 * (cc % 2) + ns]
                        v.scalar_tensor_tensor(
                            out=self.z1pre[:, ns * 512:(ns + 1) * 512],
                            in0=xo[:, ns * 512:(ns + 1) * 512], scalar=2.0,
                            in1=p[:], op0=ALU.mult, op1=ALU.add)
                    v.tensor_add(self.z1b[:], self.z1pre[:], self.projbB[:])
                    self.vsync(v, v.reduce_sum(ST[:, 32 + r:33 + r], self.z1b[:], axis=AX.X))
                    self.vsync(v, v.tensor_scalar_mul(ST[:, 36 + r:37 + r], ST[:, 32 + r:33 + r], 1.0 / D))
                    v.scalar_tensor_tensor(
                        out=self.z1pre[:], in0=self.z1b[:], scalar=ST[:, 36 + r:37 + r],
                        in1=self.z1b[:], op0=ALU.subtract, op1=ALU.mult,
                        accum_out=ST[:, 40 + r:41 + r],
                    ).then_inc(self.sV, 1)
                    v.wait_ge(self.sS, s_e10[cc])
                    v.tensor_scalar(
                        out=self.Y1TOKF[:, cc * D:(cc + 1) * D], in0=self.z1b[:],
                        scalar1=ST[:, 36 + r:37 + r], scalar2=ST[:, 48 + r:49 + r],
                        op0=ALU.subtract, op1=ALU.mult,
                    ).then_inc(self.sV, 1)
            blk.vector(_v10)
            vC = v10 + 1 + 2 * OWN

            def _s10(s):
                for cc in range(OWN):
                    r = cc % 4
                    s.wait_ge(self.sV, v_st10[cc])
                    self.ssync(s, s.activation(ST[:, 44 + r:45 + r], ST[:, 40 + r:41 + r],
                                                AF.Ln, bias=self.epst[:], scale=1.0 / D))
                    s.activation(ST[:, 48 + r:49 + r], ST[:, 44 + r:45 + r], AF.Exp,
                                 scale=-0.5).then_inc(self.sS, 1)
            blk.scalar(_s10)
            sC = s10 + OWN
            v_y1 = vC
            self.dump(blk, "d_y1", self.Y1TOKF[:], self.sV, v_y1)

            t11, v11, s11 = tC, vC, sC
            tr_mark = lambda g, k: t11 + g * 5 + k + 1
            mm_mark = lambda g: t11 + g * 5 + 5
            lt_mark = lambda tc: t11 + 160 + tc + 1
            vcast = v11 + 1
            vcp_mark = lambda g, k: vcast + g * 4 + k + 1
            vlgt_mark = lambda i: vcast + 128 + i + 1
            vsco_mark = lambda tc: vcast + 132 + tc + 1

            def _t11(t):
                t.wait_ge(self.sV, v_y1)
                for g in range(32):
                    ts4, dcx = divmod(g, 8)
                    for k in range(4):
                        tc = ts4 * 4 + k
                        i = g * 4 + k
                        if i >= 2:
                            t.wait_ge(self.sV, vcp_mark(*divmod(i - 2, 4)))
                        t.transpose(self.ptf[i % 2][:],
                                    self.Y1TOKF[:, tc * D + dcx * 128: tc * D + dcx * 128 + 128],
                                    self.eyef[:]).then_inc(self.sT, 1)
                    t.wait_ge(self.sV, vcp_mark(g, 3))
                    t.matmul(self.pgen[2 + ts4][0:8, :], self.gatets[:, dcx * 8:(dcx + 1) * 8],
                             self.strip[dcx % 2][:], start=(dcx == 0), stop=(dcx == 7)
                             ).then_inc(self.sT, 1)
                for tc in range(OWN):
                    t.wait_ge(self.sV, vlgt_mark(3))
                    if tc >= 2:
                        t.wait_ge(self.sV, vsco_mark(tc - 2))
                    t.transpose(self.ptf[tc % 2][:, 0:8],
                                self.LGT[0:8, tc * 128:(tc + 1) * 128],
                                self.eyef[0:8, 0:8]).then_inc(self.sT, 1)
            blk.tensor(_t11)
            tC = t11 + 160 + OWN

            def _v11(v):
                if DEBUG:
                    v.wait_ge(self.sST, self.stC)
                v.wait_ge(self.sV, v_y1)
                v.tensor_copy(self.Y1TB[:], self.Y1TOKF[:]).then_inc(self.sV, 1)
                for g in range(32):
                    ts4, dcx = divmod(g, 8)
                    if g >= 2:
                        v.wait_ge(self.sT, mm_mark(g - 2))
                    for k in range(4):
                        tc = ts4 * 4 + k
                        v.wait_ge(self.sT, tr_mark(g, k))
                        p = self.ptf[(g * 4 + k) % 2]
                        v.tensor_copy(self.strip[dcx % 2][:, k * 128:(k + 1) * 128], p[:])
                        v.tensor_copy(
                            self.Y1FB[:, dcx * TOK + tc * 128: dcx * TOK + tc * 128 + 128],
                            p[:]).then_inc(self.sV, 1)
                for i in range(4):
                    v.wait_ge(self.sT, mm_mark(i * 8 + 7))
                    v.tensor_copy(self.LGT[0:8, i * 512:(i + 1) * 512],
                                  self.pgen[2 + i][0:8, :]).then_inc(self.sV, 1)
                for tc in range(OWN):
                    v.wait_ge(self.sT, lt_mark(tc))
                    v.tensor_copy(self.SCO[:, tc * 8:(tc + 1) * 8],
                                  self.ptf[tc % 2][:, 0:8]).then_inc(self.sV, 1)
            blk.vector(_v11)
            vC = vcast + 128 + 4 + OWN
            v_sco = vC
            self.dump(blk, "d_lgt", self.LGT[:], self.sV, vcast + 132)

            def _s11(s):
                s.wait_ge(self.sV, v_sco)
                s.activation(self.EXPS[:], self.SCO[:], AF.Exp).then_inc(self.sS, 1)
            blk.scalar(_s11)
            sC = s11 + 1

            TK = self.TKS

            def _v11b(v):
                v.wait_ge(self.sS, s11 + 1)
                for tc in range(OWN):
                    sce = self.EXPS[:, tc * 8:(tc + 1) * 8]
                    v.tensor_max(self.TKG[:, tc * 4:(tc + 1) * 4], sce[:, 0:8:2], sce[:, 1:8:2])
                for tc in range(OWN):
                    v.reduce_max(TK[:, tc * 8:tc * 8 + 1], self.TKG[:, tc * 4:(tc + 1) * 4],
                                 axis=AX.X)
                for tc in range(OWN):
                    ge = self.TKG[:, tc * 4:(tc + 1) * 4]
                    v.scalar_tensor_tensor(out=self.TKG2[:, tc * 4:(tc + 1) * 4], in0=ge,
                                           scalar=TK[:, tc * 8:tc * 8 + 1], in1=ge,
                                           op0=ALU.is_lt, op1=ALU.mult)
                for tc in range(OWN):
                    v.reduce_max(TK[:, tc * 8 + 1:tc * 8 + 2],
                                 self.TKG2[:, tc * 4:(tc + 1) * 4], axis=AX.X)
                for tc in range(OWN):
                    v.tensor_scalar(out=self.TKG2[:, tc * 4:(tc + 1) * 4],
                                    in0=self.TKG[:, tc * 4:(tc + 1) * 4],
                                    scalar1=TK[:, tc * 8 + 1:tc * 8 + 2], scalar2=None,
                                    op0=ALU.is_ge)
                for tc in range(OWN):
                    em = self.TKE[:, tc * 8:(tc + 1) * 8]
                    gm = self.TKG2[:, tc * 4:(tc + 1) * 4]
                    v.tensor_copy(em[:, 0:8:2], gm)
                    v.tensor_copy(em[:, 1:8:2], gm)
                for tc in range(OWN):
                    v.tensor_mul(self.TKM[:, tc * 8:(tc + 1) * 8],
                                 self.EXPS[:, tc * 8:(tc + 1) * 8],
                                 self.TKE[:, tc * 8:(tc + 1) * 8])
                for tc in range(OWN):
                    v.reduce_max(TK[:, tc * 8 + 2:tc * 8 + 3],
                                 self.TKM[:, tc * 8:(tc + 1) * 8], axis=AX.X)
                for tc in range(OWN):
                    mk = self.TKM[:, tc * 8:(tc + 1) * 8]
                    v.scalar_tensor_tensor(out=self.TKM2[:, tc * 8:(tc + 1) * 8], in0=mk,
                                           scalar=TK[:, tc * 8 + 2:tc * 8 + 3], in1=mk,
                                           op0=ALU.is_lt, op1=ALU.mult)
                for tc in range(OWN):
                    v.reduce_max(TK[:, tc * 8 + 3:tc * 8 + 4],
                                 self.TKM2[:, tc * 8:(tc + 1) * 8], axis=AX.X)
                for tc in range(OWN):
                    v.tensor_add(TK[:, tc * 8 + 4:tc * 8 + 5], TK[:, tc * 8 + 2:tc * 8 + 3],
                                 TK[:, tc * 8 + 3:tc * 8 + 4])
                for tc in range(OWN):
                    v.reciprocal(TK[:, tc * 8 + 5:tc * 8 + 6], TK[:, tc * 8 + 4:tc * 8 + 5])
                for tc in range(OWN):
                    v.tensor_mul(TK[:, tc * 8 + 6:tc * 8 + 7], TK[:, tc * 8 + 2:tc * 8 + 3],
                                 TK[:, tc * 8 + 5:tc * 8 + 6])
                for tc in range(OWN):
                    v.tensor_mul(TK[:, tc * 8 + 7:tc * 8 + 8], TK[:, tc * 8 + 3:tc * 8 + 4],
                                 TK[:, tc * 8 + 5:tc * 8 + 6])
                for tc in range(OWN):
                    v.tensor_scalar(out=self.TKC1[:, tc * 8:(tc + 1) * 8],
                                    in0=self.TKM[:, tc * 8:(tc + 1) * 8],
                                    scalar1=TK[:, tc * 8 + 2:tc * 8 + 3],
                                    scalar2=TK[:, tc * 8 + 6:tc * 8 + 7],
                                    op0=ALU.is_equal, op1=ALU.mult)
                for tc in range(OWN):
                    v.tensor_scalar(out=self.TKE[:, tc * 8:(tc + 1) * 8],
                                    in0=self.TKM2[:, tc * 8:(tc + 1) * 8],
                                    scalar1=TK[:, tc * 8 + 3:tc * 8 + 4],
                                    scalar2=TK[:, tc * 8 + 7:tc * 8 + 8],
                                    op0=ALU.is_equal, op1=ALU.mult)
                for tc in range(OWN):
                    ad = v.tensor_add(self.COMB[:, tc * 8:(tc + 1) * 8],
                                      self.TKC1[:, tc * 8:(tc + 1) * 8],
                                      self.TKE[:, tc * 8:(tc + 1) * 8])
                    if tc == OWN - 1:
                        ad.then_inc(self.sV, 1)
            blk.vector(_v11b)
            vC = v_sco + 1
            v_comb = vC
            self.dump(blk, "d_comb", self.COMB[:], self.sV, v_comb)

            t12, v12, s12, qg12, qs12 = tC, vC, sC, qg, qs
            st12 = self.stC
            n_k, n_tc = 16, 8
            bank = lambda par, j: (self.pgen[j] if par == 0 else
                                   (self.pgen[4 + j] if j < 2 else self.pbank[j - 2]))

            t_marks = {}
            tt = t12
            for tb in range(NTB):
                for e in range(E):
                    for k in range(n_k):
                        tt += 1; t_marks[("gu", tb, e, k)] = tt
                    for sweep in range(4):
                        for ic in range(16):
                            tt += 1; t_marks[("w2", tb, e, sweep, ic)] = tt
            g_marks = {}
            gg = qg12
            for tb in range(NTB):
                for e in range(E):
                    for k in range(n_k):
                        gg += 16; g_marks[("w13g", tb, e, k)] = gg
                        gg += 16; g_marks[("w13u", tb, e, k)] = gg
                    for ic in range(16):
                        gg += 16; g_marks[("w2", tb, e, ic)] = gg
            vseq = []
            for tb in range(NTB):
                for e in range(E):
                    for k in range(n_k):
                        vseq.append(("act", tb, e, k))
                        if e == 0 and tb >= 1:
                            if k < n_tc:
                                vseq.append(("zst", tb - 1, k))
                            elif k == n_tc:
                                vseq.append(("var3", tb - 1, None))
                            else:
                                vseq.append(("norm", tb - 1, k - n_tc - 1))
                    if e == 0 and tb >= 1:
                        vseq.append(("norm", tb - 1, n_tc - 1))
                    for sweep in range(4):
                        vseq.append(("o2", tb, e, sweep))
                        if tb == NTB - 1 and e == E - 1:
                            vseq.append(("zst", tb, sweep * 2))
                            vseq.append(("zst", tb, sweep * 2 + 1))
            tbl = NTB - 1
            vseq.append(("var3", tbl, None))
            for tc in range(n_tc):
                vseq.append(("norm", tbl, tc))
            v_marks = {}
            vv2 = v12
            for op in vseq:
                vv2 += 1
                if op[0] == "act":
                    v_marks[("act", op[1], op[2], op[3])] = vv2
                elif op[0] == "o2":
                    v_marks[("o2", op[1], op[2], op[3])] = vv2
                elif op[0] == "var3":
                    v_marks[("var3", op[1])] = vv2
                elif op[0] == "norm":
                    v_marks[("norm", op[1], op[2])] = vv2
            s_marks = {}
            ss2 = s12
            for tb in range(NTB):
                for e in range(E):
                    for k in range(n_k):
                        ss2 += 1; s_marks[("silu", tb, e, k)] = ss2
                        if tb >= 1 and e == 0 and k == n_tc:
                            ss2 += 1; s_marks[("rs", tb - 1, 0)] = ss2
            ss2 += 1
            s_marks[("rs", tbl, 0)] = ss2
            st_marks = {}
            st_ = st12
            for tb in range(NTB):
                for tc in range(n_tc):
                    st_ += 16; st_marks[(tb, tc)] = st_

            def _g12(g):
                g.wait_ge(self.sV, v_y1)
                g.wait_ge(self.sT, t10 + 2 + OWN)
                w13cnt = 0
                for tb in range(NTB):
                    for e in range(E):
                        for k in range(n_k):
                            for half in range(2):
                                idx = w13cnt % 4
                                if w13cnt >= 4:
                                    kk = w13cnt - 4
                                    tbp, rem = divmod(kk, E * 2 * n_k)
                                    ep, rem2 = divmod(rem, 2 * n_k)
                                    kp = rem2 // 2
                                    g.wait_ge(self.sT, t_marks[("gu", tbp, ep, kp)])
                                m = k if half == 0 else 16 + k
                                g.dma_start(self.W13B[idx][:], self.w13b[e, m, :, :]
                                            ).then_inc(self.sQG, 16)
                                w13cnt += 1
                        if (tb, e) != (0, 0):
                            pe = tb * E + e - 1
                            ptb, pex = divmod(pe, E)
                            g.wait_ge(self.sT, t_marks[("w2", ptb, pex, 3, 15)])
                        for ic in range(16):
                            g.dma_start(self.W2ALL[:, ic * 1024:(ic + 1) * 1024],
                                        self.w2t[e, ic * 128:(ic + 1) * 128, :]
                                        ).then_inc(self.sQG, 16)
            blk.gpsimd(_g12)
            qg = gg

            def _t12(t):
                t.wait_ge(self.sV, vcast + 132 + OWN)
                for tb in range(NTB):
                    for e in range(E):
                        for k in range(n_k):
                            par = k % 2
                            if k >= 2:
                                t.wait_ge(self.sV, v_marks[("act", tb, e, k - 2)])
                            elif (tb, e) != (0, 0):
                                pe = tb * E + e - 1
                                ptb, pex = divmod(pe, E)
                                t.wait_ge(self.sV, v_marks[("o2", ptb, pex, 2 + k)])
                            for half in range(2):
                                gm = g_marks[("w13g" if half == 0 else "w13u", tb, e, k)]
                                t.wait_ge(self.sQG, gm)
                                blkidx = ((tb * E + e) * n_k + k) * 2 + half
                                for ts in range(2):
                                    p = bank(par, 2 * half + ts)
                                    for dcx in range(8):
                                        lhs = self.W13B[blkidx % 4][:, dcx * 128:(dcx + 1) * 128]
                                        rhs = self.Y1FB[:, dcx * TOK + tb * TB + ts * 512:
                                                        dcx * TOK + tb * TB + ts * 512 + 512]
                                        mm = t.matmul(p[:], lhs, rhs,
                                                      start=(dcx == 0), stop=(dcx == 7))
                                        if half == 1 and ts == 1 and dcx == 7:
                                            mm.then_inc(self.sT, 1)
                        for sweep in range(4):
                            par = sweep % 2
                            if sweep == 0:
                                t.wait_ge(self.sV, v_marks[("act", tb, e, n_k - 2)])
                            elif sweep == 1:
                                t.wait_ge(self.sV, v_marks[("act", tb, e, n_k - 1)])
                            else:
                                t.wait_ge(self.sV, v_marks[("o2", tb, e, sweep - 2)])
                            for ic in range(16):
                                t.wait_ge(self.sQG, g_marks[("w2", tb, e, ic)])
                                for tci in range(2):
                                    tc = sweep * 2 + tci
                                    for ns in range(2):
                                        p = bank(par, 2 * tci + ns)
                                        lhs = self.ACT[:, ic * TB + tc * 128:
                                                       ic * TB + tc * 128 + 128]
                                        rhs = self.W2ALL[:, ic * 1024 + ns * 512:
                                                         ic * 1024 + (ns + 1) * 512 if ns else
                                                         ic * 1024 + 512]
                                        rhs = self.W2ALL[:, ic * 1024 + ns * 512:
                                                         ic * 1024 + ns * 512 + 512]
                                        mm = t.matmul(p[:], lhs, rhs,
                                                      start=(ic == 0), stop=(ic == 15))
                                        if tci == 1 and ns == 1:
                                            mm.then_inc(self.sT, 1)
            blk.tensor(_t12)
            tC = tt

            def _v12(v):
                if DEBUG:
                    v.wait_ge(self.sST, self.stC)
                fin = self.OUT2[E % 2]
                zb = self.OUT2[1 - (E % 2)]
                for op in vseq:
                    if op[0] == "act":
                        _, tb, e, k = op
                        par = k % 2
                        v.wait_ge(self.sS, s_marks[("silu", tb, e, k)])
                        for ts in range(2):
                            pup = bank(par, 2 + ts)
                            mu = v.tensor_mul(
                                self.ACT[:, k * TB + ts * 512: k * TB + ts * 512 + 512],
                                self.SG[:, par * 1024 + ts * 512: par * 1024 + (ts + 1) * 512],
                                pup[:])
                            if ts == 1:
                                mu.then_inc(self.sV, 1)
                    elif op[0] == "o2":
                        _, tb, e, sweep = op
                        par = sweep % 2
                        v.wait_ge(self.sT, t_marks[("w2", tb, e, sweep, 15)])
                        for tci in range(2):
                            tc = sweep * 2 + tci
                            ccol = self.COMB[:, (tb * 8 + tc) * 8 + e:
                                             (tb * 8 + tc) * 8 + e + 1]
                            for ns in range(2):
                                p = bank(par, 2 * tci + ns)
                                dst = self.OUT2[(e + 1) % 2][:, tc * D + ns * 512:
                                                             tc * D + ns * 512 + 512]
                                if e == 0:
                                    o = v.tensor_scalar(out=dst, in0=p[:], scalar1=ccol,
                                                        scalar2=None, op0=ALU.mult)
                                else:
                                    o = v.scalar_tensor_tensor(
                                        out=dst, in0=p[:], scalar=ccol,
                                        in1=self.OUT2[e % 2][:, tc * D + ns * 512:
                                                             tc * D + ns * 512 + 512],
                                        op0=ALU.mult, op1=ALU.add)
                                if tci == 1 and ns == 1:
                                    o.then_inc(self.sV, 1)
                    elif op[0] == "zst":
                        _, tbn, tc = op
                        v.tensor_add(zb[:, tc * D:(tc + 1) * D],
                                     self.Y1TB[:, (tbn * 8 + tc) * D:(tbn * 8 + tc + 1) * D],
                                     fin[:, tc * D:(tc + 1) * D])
                        v.reduce_sum(ST[:, tc:tc + 1], zb[:, tc * D:(tc + 1) * D], axis=AX.X)
                        v.scalar_tensor_tensor(
                            out=self.ln3scr[:], in0=zb[:, tc * D:(tc + 1) * D], scalar=0.0,
                            in1=zb[:, tc * D:(tc + 1) * D], op0=ALU.add, op1=ALU.mult,
                            accum_out=ST[:, 8 + tc:9 + tc]).then_inc(self.sV, 1)
                    elif op[0] == "var3":
                        self.vsync(v, v.tensor_scalar_mul(ST[:, 16:24], ST[:, 0:8], -1.0 / D))
                        v.tensor_mul(ST[:, 32:40], ST[:, 16:24], ST[:, 16:24])
                        self.vsync(v, v.tensor_scalar_mul(ST[:, 40:48], ST[:, 8:16], 1.0 / D))
                        v.tensor_sub(ST[:, 8:16], ST[:, 40:48], ST[:, 32:40]
                                     ).then_inc(self.sV, 1)
                    else:
                        _, tbn, tc = op
                        if tc == 0:
                            v.wait_ge(self.sS, s_marks[("rs", tbn, 0)])
                        if tbn > 0 or tc >= 2:
                            ptb, ptc = (tbn, tc - 2) if tc >= 2 else (tbn - 1, n_tc + tc - 2)
                            v.wait_ge(self.sST, st_marks[(ptb, ptc)])
                        v.tensor_scalar(
                            out=self.OUTB[tc % 2][:], in0=zb[:, tc * D:(tc + 1) * D],
                            scalar1=ST[:, 16 + tc:17 + tc], scalar2=ST[:, 24 + tc:25 + tc],
                            op0=ALU.add, op1=ALU.mult,
                        ).then_inc(self.sV, 1)
            blk.vector(_v12)
            vC = vv2

            def _s12(s):
                def rs_pair(tbn):
                    s.wait_ge(self.sV, v_marks[("var3", tbn)])
                    self.ssync(s, s.activation(ST[:, 40:48], ST[:, 8:16],
                                               AF.Ln, bias=self.epst[:], scale=1.0))
                    s.activation(ST[:, 24:32], ST[:, 40:48], AF.Exp,
                                 scale=-0.5).then_inc(self.sS, 1)
                for tb in range(NTB):
                    for e in range(E):
                        for k in range(n_k):
                            par = k % 2
                            s.wait_ge(self.sT, t_marks[("gu", tb, e, k)])
                            for ts in range(2):
                                pg = bank(par, 0 + ts)
                                ac = s.activation(
                                    self.SG[:, par * 1024 + ts * 512: par * 1024 + (ts + 1) * 512],
                                    pg[:], AF.Silu)
                                if ts == 1:
                                    ac.then_inc(self.sS, 1)
                            if tb >= 1 and e == 0 and k == n_tc:
                                rs_pair(tb - 1)
                rs_pair(tbl)
            blk.scalar(_s12)
            sC = ss2

            def _sy12(sy):
                for tb in range(NTB):
                    for tc in range(n_tc):
                        sy.wait_ge(self.sV, v_marks[("norm", tb, tc)])
                        row = (tb * 8 + tc) * 128
                        sy.dma_start(self.out[row:row + 128, :], self.OUTB[tc % 2][:]
                                     ).then_inc(self.sST, 16)
                sy.wait_ge(self.sST, st12 + NTB * n_tc * 16)
            blk.sync(_sy12)
            self.stC = st12 + NTB * n_tc * 16
        return self.nc


def _stage_inputs(L0_attn_x, L0_attn_norm_w, L0_attn_norm_b, L0_attn_qkv_w,
                  L0_attn_qkv_b, L0_attn_proj_w, L0_attn_proj_b, L0_ffn_gate_w,
                  L0_ffn_w13_weight, L0_ffn_w2_weight):
    x = np.asarray(L0_attn_x, dtype=np.float32)
    qkvT = np.ascontiguousarray(np.asarray(L0_attn_qkv_w, np.float32).T)
    qkvb = np.ascontiguousarray(np.asarray(L0_attn_qkv_b, np.float32).reshape(1, 3 * D))
    nw = np.ascontiguousarray(np.asarray(L0_attn_norm_w, np.float32).reshape(8, 128).T)
    nb = np.ascontiguousarray(np.asarray(L0_attn_norm_b, np.float32).reshape(8, 128).T)
    projT = np.ascontiguousarray(np.asarray(L0_attn_proj_w, np.float32).T)
    projb = np.ascontiguousarray(np.asarray(L0_attn_proj_b, np.float32).reshape(1, D))
    gateT = np.ascontiguousarray(np.asarray(L0_ffn_gate_w, np.float32).T)
    w13 = np.asarray(L0_ffn_w13_weight, np.float32)
    w13b = np.ascontiguousarray(
        w13.transpose(0, 2, 1).reshape(E, 8, 128, 32, 128).transpose(0, 3, 2, 1, 4)
        .reshape(E, 32, 128, 1024))
    w2t = np.ascontiguousarray(np.asarray(L0_ffn_w2_weight, np.float32).transpose(0, 2, 1))
    eye = np.eye(128, dtype=np.float32)
    shared = dict(qkvT=qkvT, qkvb=qkvb, nw=nw, nb=nb, projT=projT, projb=projb,
                  gateT=gateT, w13b=w13b, w2t=w2t, eye=eye)
    in_maps = []
    for c in range(NCORES):
        n, q = divmod(c, 4)
        xr = np.ascontiguousarray(np.roll(x[n], -q * TOK, axis=0))
        m = dict(shared)
        m["xfull"] = xr
        in_maps.append(m)
    return in_maps


def kernel(**inputs):
    in_maps = _stage_inputs(**inputs)
    mk = MK()
    nc = mk.build()
    res = run_bass_kernel_spmd(nc, in_maps, core_ids=list(range(NCORES)))
    out = np.empty((B, L, D), dtype=np.float32)
    for c in range(NCORES):
        n, q = divmod(c, 4)
        out[n, q * TOK:(q + 1) * TOK, :] = res.results[c]["out"]
    return out
